# version 03
# baseline (speedup 1.0000x reference)
from contextlib import ExitStack
import numpy as np
import ml_dtypes
import concourse.bass as bass
import concourse.mybir as mybir
from concourse.bass_utils import run_bass_kernel_spmd

F32 = mybir.dt.float32
BF16 = mybir.dt.bfloat16
AF = mybir.ActivationFunctionType
ALU = mybir.AluOpType
AX = mybir.AxisListType

S = 4096
DM = 1024
NP = 8
W = 16
SCALE = 0.125
FF = 2048
NE = 8
STRIPES = ([0, 3, 5, 6], [1, 2, 4, 7])
BF = ml_dtypes.bfloat16

_DEBUG = ()


def _rope_tables():
    inv = 1.0 / 10000.0 ** (np.arange(0, 16, 2, dtype=np.float64) / 16.0)
    t = np.arange(S, dtype=np.float64)
    fr = np.outer(inv, t)
    C = np.ones((128, S), dtype=np.float64)
    Sn = np.zeros((128, S), dtype=np.float64)
    for base in (0, 64):
        C[base:base + 8] = np.cos(fr)
        C[base + 8:base + 16] = np.cos(fr)
        Sn[base:base + 8] = -np.sin(fr)
        Sn[base + 8:base + 16] = np.sin(fr)
    return C.astype(np.float32).astype(BF), Sn.astype(np.float32).astype(BF)


def _win_masks():
    kk = np.arange(128)[:, None]
    qq = np.arange(128)[None, :]
    m_same = ((qq - kk >= 0) & (qq - kk <= W - 1)).astype(np.float32)
    m_prev = ((qq + 128 - kk >= 0) & (qq + 128 - kk <= W - 1)).astype(np.float32)
    return m_same.astype(BF), m_prev.astype(BF)


def _diag_masks():
    kk = np.arange(128)[:, None]
    qq = np.arange(512)[None, :]
    out = np.zeros((4, 128, 512), dtype=np.float32)
    for cr in range(4):
        out[cr] = (128 * cr + kk <= qq).astype(np.float32)
    return out.astype(BF)


ROT_MASK = [(i + 8) % 16 if (i % 32) < 16 else i for i in range(32)]


class Ctr:
    def __init__(self, ctx, nc, name):
        self.h = ctx.enter_context(nc.semaphore(name))
        self.n = 0

    def inc(self, instr, k=1):
        instr.then_inc(self.h, k)
        self.n += k
        return instr

    def wait(self, eng, val=None):
        eng.wait_ge(self.h, self.n if val is None else val)


def build_program(debug=(), upto=3, p1_pairs=NP, p1_win=True, p1_rope=True, p1_v=True, p1_qk=True):
    nc = bass.Bass("TRN2", target_bir_lowering=False, detect_race_conditions=False)

    d = {}
    def din(name, shape):
        d[name] = nc.dram_tensor(name, list(shape), F32, kind="ExternalInput").ap()
    din("x_b", (S, DM))
    din("x_mine", (2048, DM))
    din("wqkvT", (DM, 3 * DM))
    din("bqkv_pc", (128, 24))
    din("bqkv_row", (1, 3 * DM))
    din("wqaT", (DM, DM))
    din("bqa_pc", (128, 8))
    din("wkaT", (DM, DM))
    din("bka_pc", (128, 8))
    din("wdT", (DM, DM))
    din("bd_row", (1, DM))
    din("router_w_pc", (128, 8, NE))
    din("router_b_row", (1, NE))
    din("sw1", (DM, FF))
    din("sb1_pc", (128, 16))
    din("sw2", (FF, DM))
    din("sb2_row", (1, DM))
    din("rw1", (NE, DM, FF))
    din("rb1_pc", (128, NE, 16))
    din("rw2", (NE, FF, DM))
    din("rb2", (NE, DM))
    out_h = nc.dram_tensor("out", [2048, DM], F32, kind="ExternalOutput").ap()

    dbg = {}
    for nm in debug:
        shp = {"d_anat": [NP, 128, 32 * 160], "d_aht": [128, NP, S],
               "d_xnt": [128, NP, S], "d_outT": [128, NP, 2048],
               "d_kat": [128, S], "d_qat": [128, 4 * 512],
               "d_gates": [128, 8, 9], "d_yacc": [128, 8, DM]}[nm]
        dbg[nm] = nc.dram_tensor(nm, shp, F32, kind="ExternalOutput").ap()

    xn1_h = nc.dram_tensor("xn1_h", [S, DM], BF16, kind="Internal").ap()
    anat_h = nc.dram_tensor("anat_h", [NP, 128, 32 * 160], BF16, kind="Internal").ap()
    anatP_h = nc.dram_tensor("anatP_h", [NP, S, 128], BF16, kind="Internal").ap()
    outT_h = nc.dram_tensor("outT_h", [128, NP, 2048], BF16, kind="Internal").ap()
    xn2_h = nc.dram_tensor("xn2_h", [2048, DM], BF16, kind="Internal").ap()

    COSt, SINt = _rope_tables()
    MSAME, MPREV = _win_masks()
    DIAG = _diag_masks()
    cos_t = nc.inline_tensor(np.ascontiguousarray(COSt), name="cos_t").ap()
    sin_t = nc.inline_tensor(np.ascontiguousarray(SINt), name="sin_t").ap()
    msame_t = nc.inline_tensor(np.ascontiguousarray(MSAME), name="msame_t").ap()
    mprev_t = nc.inline_tensor(np.ascontiguousarray(MPREV), name="mprev_t").ap()
    diag_t = nc.inline_tensor(np.ascontiguousarray(DIAG), name="diag_t").ap()
    ones128_t = nc.inline_tensor(np.ones((1, 128), dtype=BF), name="ones128_t").ap()
    ones64_t = nc.inline_tensor(np.ones((1, 64), dtype=BF), name="ones64_t").ap()

    L0 = ExitStack()
    with L0:
        sbuf = lambda name, shape, dt=BF16: L0.enter_context(
            nc.sbuf_tensor(name, list(shape), dt))
        cos_s = sbuf("cos_s", (128, S))
        sin_s = sbuf("sin_s", (128, S))
        msame_s = sbuf("msame_s", (128, 128))
        mprev_s = sbuf("mprev_s", (128, 128))
        diag_s = sbuf("diag_s", (128, 4, 512))
        o128 = sbuf("o128", (1, 128))
        o64 = sbuf("o64", (1, 64))
        bq_sb = sbuf("bq_sb", (128, 24), F32)
        bqa_sb = sbuf("bqa_sb", (128, 8), F32)
        bka_sb = sbuf("bka_sb", (128, 8), F32)
        sb1_sb = sbuf("sb1_sb", (128, 16), F32)
        rb1_sb = sbuf("rb1_sb", (128, NE, 16), F32)
        rwr_sb = sbuf("rwr_sb", (128, 8, NE))
        gates_sb = sbuf("gates_sb", (128, 8, 9), F32)

        pb = [L0.enter_context(nc.psum_tensor(f"pb{i}", [128, 512], F32))
              for i in range(8)]

        qL = Ctr(L0, nc, "qL")
        qG = Ctr(L0, nc, "qG")
        qS = Ctr(L0, nc, "qS")
        tPS = Ctr(L0, nc, "tPS")
        tSC = Ctr(L0, nc, "tSC")
        tPV = Ctr(L0, nc, "tPV")
        tBC = Ctr(L0, nc, "tBC")
        sEV = Ctr(L0, nc, "sEV")
        sEX = Ctr(L0, nc, "sEX")
        sLN = Ctr(L0, nc, "sLN")
        vIN = Ctr(L0, nc, "vIN")
        vRP = Ctr(L0, nc, "vRP")
        vMS = Ctr(L0, nc, "vMS")
        vNM = Ctr(L0, nc, "vNM")
        vHL = Ctr(L0, nc, "vHL")
        vEV = Ctr(L0, nc, "vEV")
        CTRS = [qL, qG, qS, tPS, tSC, tPV, tBC, sEV, sEX, sLN,
                vIN, vRP, vMS, vNM, vHL, vEV]

        block = L0.enter_context(nc.Block())
        par_reg = {}

        def setup_parity(eng):
            pid = eng.alloc_register("pid_" + eng.engine.name)
            par = eng.alloc_register("par_" + eng.engine.name)
            eng.reg_mov(pid, eng.partition_id())
            eng.reg_mod(par, pid, 2)
            par_reg[eng.engine] = par

        @block.sync
        def _(s):
            for dst, src in ((cos_s, cos_t), (sin_s, sin_t), (msame_s, msame_t),
                             (mprev_s, mprev_t), (o128, ones128_t), (o64, ones64_t)):
                qL.inc(s.dma_start(out=dst[:], in_=src), 16)
            for cr in range(4):
                qL.inc(s.dma_start(out=diag_s[:, cr, :], in_=diag_t[cr]), 16)
            qL.inc(s.dma_start(out=bq_sb[:], in_=d["bqkv_pc"]), 16)
            qL.inc(s.dma_start(out=bqa_sb[:], in_=d["bqa_pc"]), 16)
            qL.inc(s.dma_start(out=bka_sb[:], in_=d["bka_pc"]), 16)
            qL.inc(s.dma_start(out=sb1_sb[:], in_=d["sb1_pc"]), 16)
            qL.inc(s.dma_start(out=rb1_sb[:], in_=d["rb1_pc"]), 16)
        qL_P0 = qL.n

        @block.gpsimd
        def _(g):
            qG.inc(g.dma_start(out=rwr_sb[:], in_=d["router_w_pc"]), 16)
        qG_P0 = qG.n

        @block.vector
        def _(v):
            setup_parity(v)
            vIN.inc(v.memset(gates_sb[:, :, 8:9], 1.0))

        @block.tensor
        def _(t):
            setup_parity(t)

        @block.scalar
        def _(sc):
            setup_parity(sc)


        qkv_end = []
        anat_qs = []

        ctx1 = ExitStack()
        with ctx1:
            with ExitStack() as ctxLN:
                c1 = lambda name, shape, dt: ctxLN.enter_context(
                    nc.sbuf_tensor(name, list(shape), dt))
                xb = c1("xb", (128, 4, DM), F32)
                xnb = c1("xnb", (128, 4, DM), BF16)
                trash = c1("trash", (128, 4, DM), BF16)
                mcol = c1("mcol", (128, 4), F32)
                msqc = c1("msqc", (128, 4), F32)
                ssqc = c1("ssqc", (128, 4), F32)
                varc = c1("varc", (128, 4), F32)
                sqvc = c1("sqvc", (128, 4), F32)
                rstdc = c1("rstdc", (128, 4), F32)

                x_src = d["x_b"].rearrange("(t p) m -> p t m", p=128)
                xn_dst = xn1_h.rearrange("(t p) m -> p t m", p=128)
                qL0 = qL.n
                vEV0 = vEV.n
                qS0 = qS.n

                @block.sync
                def _(s):
                    for t in range(32):
                        if t >= 4:
                            s.wait_ge(vEV.h, vEV0 + t - 3)
                        qL.inc(s.dma_start(out=xb[:, t % 4, :], in_=x_src[:, t, :]), 16)

                for gidx in range(8):
                    tiles = [4 * gidx + i for i in range(4)]
                    sLN0, vMS0 = sLN.n, vMS.n
                    tl0 = qL0 + 16 * (tiles[-1] + 1)

                    @block.scalar
                    def _(sc, tiles=tiles, tl0=tl0):
                        sc.wait_ge(qL.h, tl0)
                        for i, t in enumerate(tiles):
                            sLN.inc(sc.activation(trash[:, i, :], xb[:, t % 4, :], AF.Square,
                                                  accum_out=ssqc[:, i:i + 1]))

                    @block.vector
                    def _(v, tiles=tiles, tl0=tl0, sLN0=sLN0):
                        v.wait_ge(qL.h, tl0)
                        for i, t in enumerate(tiles):
                            v.reduce_sum(mcol[:, i:i + 1], xb[:, t % 4, :], axis=AX.X)
                        for i in range(4):
                            v.tensor_scalar_mul(mcol[:, i:i + 1], mcol[:, i:i + 1], 1.0 / DM)
                        for i in range(4):
                            v.tensor_tensor(msqc[:, i:i + 1], mcol[:, i:i + 1],
                                            mcol[:, i:i + 1], ALU.mult)
                        v.wait_ge(sLN.h, sLN0 + 4)
                        for i in range(4):
                            v.scalar_tensor_tensor(varc[:, i:i + 1], ssqc[:, i:i + 1],
                                                   1.0 / DM, msqc[:, i:i + 1],
                                                   ALU.mult, ALU.subtract)
                        for i in range(4):
                            vMS.inc(v.tensor_scalar_add(varc[:, i:i + 1], varc[:, i:i + 1], 1e-5))

                    @block.scalar
                    def _(sc, vMS0=vMS0):
                        sc.wait_ge(vMS.h, vMS0 + 4)
                        for i in range(4):
                            sLN.inc(sc.activation(sqvc[:, i:i + 1], varc[:, i:i + 1], AF.Sqrt))

                    @block.vector
                    def _(v, tiles=tiles, sLN0=sLN0, qS0=qS0):
                        v.wait_ge(sLN.h, sLN0 + 8)
                        for i in range(4):
                            v.reciprocal(rstdc[:, i:i + 1], sqvc[:, i:i + 1])
                        for i, t in enumerate(tiles):
                            if t >= 4:
                                v.wait_ge(qS.h, qS0 + 16 * (t - 3))
                            vEV.inc(v.tensor_scalar(xnb[:, t % 4, :], xb[:, t % 4, :],
                                                    mcol[:, i:i + 1], rstdc[:, i:i + 1],
                                                    ALU.subtract, ALU.mult))

                    vEVg = vEV.n

                    @block.gpsimd
                    def _(g, tiles=tiles, vEVg=vEVg):
                        for i, t in enumerate(tiles):
                            g.wait_ge(vEV.h, vEVg - 3 + i)
                            qS.inc(g.dma_start(out=xn_dst[:, t, :], in_=xnb[:, t % 4, :]), 16)

            xnT = ctx1.enter_context(nc.sbuf_tensor("xnT", [128, 8, S], BF16))
            qS_ln = qS.n

            @block.sync
            def _(s):
                s.wait_ge(qS.h, qS_ln)
                for f in range(8):
                    qL.inc(s.dma_start(out=xnT[:, f, :],
                                       in_=xn1_h[:, 128 * f:128 * f + 128],
                                       transpose=True), 16)
            qL_xnT = qL.n

            if upto < 1:
                if "d_xnt" in dbg:
                    @block.gpsimd
                    def _(g):
                        g.wait_ge(qL.h, qL_xnT)
                        qS.inc(g.dma_start(out=dbg["d_xnt"], in_=xnT[:]), 16)

                @block.gpsimd
                def _(g):
                    qS.wait(g)
                    qL.wait(g)
                return nc

            rows1_sb = ctx1.enter_context(nc.sbuf_tensor("rows1_sb", [1, 3072], BF16))

            @block.gpsimd
            def _(g):
                qG.inc(g.dma_start(out=rows1_sb[:], in_=d["bqkv_row"]), 16)

            wq_b = ctx1.enter_context(nc.sbuf_tensor("wq_b", [128, 2, 8, 128], BF16))
            wk_b = ctx1.enter_context(nc.sbuf_tensor("wk_b", [128, 2, 8, 128], BF16))
            wv_b = ctx1.enter_context(nc.sbuf_tensor("wv_b", [128, 2, 8, 128], BF16))
            qTp = ctx1.enter_context(nc.sbuf_tensor("qTp", [128, S], BF16))
            kTp = ctx1.enter_context(nc.sbuf_tensor("kTp", [128, S], BF16))
            vnat = ctx1.enter_context(nc.sbuf_tensor("vnat", [128, 32, 160], BF16))
            anat = ctx1.enter_context(nc.sbuf_tensor("anat", [128, 32, 160], BF16))
            rot_b = ctx1.enter_context(nc.sbuf_tensor("rot_b", [128, 2, 1024], BF16))
            t2f = ctx1.enter_context(nc.sbuf_tensor("t2f", [128, 2, 1024], F32))
            t3f = ctx1.enter_context(nc.sbuf_tensor("t3f", [128, 2, 1024], F32))
            PT = ctx1.enter_context(nc.sbuf_tensor("PT", [128, 4, 128], BF16))
            rcol = ctx1.enter_context(nc.sbuf_tensor("rcol", [128, 2], F32))

            wsrc = d["wqkvT"].rearrange("(c p) f -> p c f", p=128)

            for j in range(p1_pairs):
                jj = j % 2
                qG0 = qG.n

                @block.gpsimd
                def _(g, j=j, jj=jj):
                    if j >= 2:
                        g.wait_ge(tPS.h, qkv_end[j - 2])
                    qG.inc(g.dma_start(out=wq_b[:, jj, :, :],
                                       in_=wsrc[:, :, 128 * j:128 * j + 128]), 16)
                    qG.inc(g.dma_start(out=wk_b[:, jj, :, :],
                                       in_=wsrc[:, :, 1024 + 128 * j:1024 + 128 * j + 128]), 16)
                    qG.inc(g.dma_start(out=wv_b[:, jj, :, :],
                                       in_=wsrc[:, :, 2048 + 128 * j:2048 + 128 * j + 128]), 16)

                qS_prev = qS.n

                @block.vector
                def _(v, qS_prev=qS_prev):
                    v.wait_ge(qS.h, qS_prev)
                    v.memset(vnat[:], 0.0)
                    v.memset(anat[:], 0.0)
                    v.memset(vnat[:, :, 64:65], 1.0)
                    v.memset(vnat[:, :, 144:145], 1.0)
                    v.memset(anat[:, :, 64:65], 1.0)
                    vIN.inc(v.memset(anat[:, :, 144:145], 1.0))
                vIN_j = vIN.n

                sEV0, tPV0 = sEV.n, tPV.n
                q_targets = []

                @block.tensor
                def _(t, j=j, jj=jj, qG0=qG0, sEV0=sEV0):
                    t.wait_ge(qL.h, qL_xnT)
                    t.wait_ge(qG.h, qG0 + 48)
                    vNM.wait(t)
                    for op, wbuf in (((0, wq_b), (1, wk_b)) if p1_qk else ()):
                        for blk in range(8):
                            gi = op * 8 + blk
                            if gi >= 2:
                                t.wait_ge(sEV.h, sEV0 + gi - 1)
                            ps = pb[gi % 2]
                            for dc in range(8):
                                t.matmul(ps[:, 0:512], wbuf[:, jj, dc, :],
                                         xnT[:, dc, 512 * blk:512 * blk + 512],
                                         start=(dc == 0), stop=(dc == 7))
                            tPS.inc(t.drain())
                            q_targets.append(tPS.n)
                    if p1_v:
                        for tt in range(32):
                            gi = 16 + tt
                            t.wait_ge(sEV.h, sEV0 + gi - 1)
                            ps = pb[gi % 2]
                            for dc in range(8):
                                t.matmul(ps[:, 0:128], xnT[:, dc, 128 * tt:128 * tt + 128],
                                         wv_b[:, jj, dc, :], start=(dc == 0), stop=False)
                            t.matmul(ps[:, 0:128], o128[:],
                                     rows1_sb[:, 2048 + 128 * j:2048 + 128 * j + 128],
                                     start=False, stop=True)
                            tPS.inc(t.drain())
                            q_targets.append(tPS.n)
                qkv_end.append(tPS.n)

                @block.scalar
                def _(sc, j=j, vIN_j=vIN_j):
                    it = iter(q_targets)
                    if p1_qk:
                        for op, dst, bcol in ((0, qTp, j), (1, kTp, 8 + j)):
                            for blk in range(8):
                                sc.wait_ge(tPS.h, next(it))
                                ps = pb[(op * 8 + blk) % 2]
                                sEV.inc(sc.activation(dst[:, 512 * blk:512 * blk + 512],
                                                      ps[:, 0:512], AF.Identity,
                                                      bias=bq_sb[:, bcol:bcol + 1]))
                    sc.wait_ge(vIN.h, vIN_j)
                    if p1_v:
                        for tt in range(32):
                            sc.wait_ge(tPS.h, next(it))
                            ps = pb[tt % 2]
                            dst = vnat[:, tt, :].rearrange("p (b c) -> p b c", b=2)[:, :, 0:64]
                            src = ps[:, 0:128].rearrange("p (b c) -> p b c", b=2)
                            sEV.inc(sc.activation(dst, src, AF.Identity))

                sEV_qk = sEV0 + (16 if p1_qk else 0)

                @block.vector
                def _(v, sEV_qk=sEV_qk):
                    if not p1_rope:
                        vRP.inc(v.memset(rcol[:], 0.0))
                        vRP.inc(v.memset(rcol[:], 0.0))
                        return
                    v.wait_ge(sEV.h, sEV_qk)
                    ins = None
                    for dst in (qTp, kTp):
                        for bp in range(2):
                            for hb in range(2):
                                blk = 2 * bp + hb
                                v.stream_shuffle(rot_b[:, hb, :],
                                                 dst[:, 1024 * blk:1024 * blk + 1024], ROT_MASK)
                            for hb in range(2):
                                blk = 2 * bp + hb
                                v.tensor_tensor(t2f[:, hb, :], rot_b[:, hb, :],
                                                sin_s[:, 1024 * blk:1024 * blk + 1024], ALU.mult)
                            for hb in range(2):
                                blk = 2 * bp + hb
                                v.tensor_tensor(t3f[:, hb, :],
                                                dst[:, 1024 * blk:1024 * blk + 1024],
                                                cos_s[:, 1024 * blk:1024 * blk + 1024], ALU.mult)
                            for hb in range(2):
                                blk = 2 * bp + hb
                                ins = v.tensor_tensor(dst[:, 1024 * blk:1024 * blk + 1024],
                                                      t2f[:, hb, :], t3f[:, hb, :], ALU.add)
                        vRP.inc(ins)
                vRP_j = vRP.n

                if not p1_win:
                    continue
                sEX0, vMS0, vNM0 = sEX.n, vMS.n, vNM.n
                pv_targets = []
                sc_targets = []

                @block.tensor
                def _(t, j=j, vRP_j=vRP_j, sEX0=sEX0, vMS0=vMS0, vNM0=vNM0):
                    t.wait_ge(vRP.h, vRP_j)
                    t.wait_ge(vNM.h, vNM0)
                    t.wait_ge(sEX.h, sEX0)
                    n = 0
                    for qc in range(32):
                        kcs = ([qc - 1] if qc > 0 else []) + [qc]
                        qtiles = []
                        for kc in kcs:
                            for head in range(2):
                                if n >= 4:
                                    t.wait_ge(sEX.h, sEX0 + n - 3)
                                bank = pb[2 + n % 4]
                                t.matmul(bank[:, 0:128],
                                         kTp[64 * head:64 * head + 64, 128 * kc:128 * kc + 128],
                                         qTp[64 * head:64 * head + 64, 128 * qc:128 * qc + 128],
                                         start=True, stop=True,
                                         tile_position=(64 * head, 0))
                                qtiles.append((n, kc, head))
                                n += 1
                            tSC.inc(t.drain())
                            sc_targets.append(tSC.n)
                            sc_targets.append(tSC.n)
                        outb = [pb[6], pb[7]] if qc % 2 == 0 else [pb[0], pb[1]]
                        for bidx in range(len(qtiles) // 2):
                            pair = qtiles[2 * bidx:2 * bidx + 2]
                            for tn, kc, head in pair:
                                t.wait_ge(vMS.h, vMS0 + tn + 1)
                            first = bidx == 0
                            last = bidx == len(qtiles) // 2 - 1
                            if first and qc >= 2:
                                t.wait_ge(vNM.h, vNM0 + 2 * qc - 2)
                            for tn, kc, head in pair:
                                t.matmul(outb[head][:, 0:65], PT[:, tn % 4, :],
                                         vnat[:, kc, 80 * head:80 * head + 65],
                                         start=first, stop=last)
                            tPV.inc(t.drain())
                            pv_targets.append(tPV.n)
                            pv_targets.append(tPV.n)
                ntl = len(sc_targets)

                @block.scalar
                def _(sc, tPV0=tPV0):
                    for n in range(ntl):
                        sc.wait_ge(tSC.h, sc_targets[n])
                        sc.wait_ge(tPV.h, tPV0 if n < 4 else pv_targets[n - 4])
                        bank = pb[2 + n % 4]
                        sEX.inc(sc.activation(PT[:, n % 4, :], bank[:, 0:128],
                                              AF.Exp, scale=SCALE))

                @block.vector
                def _(v, j=j, sEX0=sEX0):
                    n = 0
                    for qc in range(32):
                        kcs = ([qc - 1] if qc > 0 else []) + [qc]
                        for kc in kcs:
                            for head in range(2):
                                v.wait_ge(sEX.h, sEX0 + n + 1)
                                mask = msame_s if kc == qc else mprev_s
                                vMS.inc(v.tensor_tensor(PT[:, n % 4, :], PT[:, n % 4, :],
                                                        mask[:], ALU.mult))
                                n += 1
                        outb = [pb[6], pb[7]] if qc % 2 == 0 else [pb[0], pb[1]]
                        v.wait_ge(tPV.h, pv_targets[n - 1])
                        for head in range(2):
                            v.reciprocal(rcol[:, head:head + 1], outb[head][:, 64:65])
                        for head in range(2):
                            vNM.inc(v.tensor_scalar(anat[:, qc, 80 * head:80 * head + 64],
                                                    outb[head][:, 0:64],
                                                    rcol[:, head:head + 1], None, ALU.mult))
                vNM_j = vNM.n

                @block.gpsimd
                def _(g, j=j, vNM_j=vNM_j):
                    g.wait_ge(vNM.h, vNM_j)
                    qS.inc(g.dma_start(out=anat_h[j].rearrange("p (t c) -> p t c", c=160),
                                       in_=anat[:]), 16)
                    dstP = anatP_h[j].rearrange("(t p) c -> p t c", p=128)
                    for q4 in range(4):
                        qS.inc(g.dma_start(out=dstP[:, 8 * q4:8 * q4 + 8, 0:64],
                                           in_=anat[:, 8 * q4:8 * q4 + 8, 0:64]), 16)
                        qS.inc(g.dma_start(out=dstP[:, 8 * q4:8 * q4 + 8, 64:128],
                                           in_=anat[:, 8 * q4:8 * q4 + 8, 80:144]), 16)
                    if "d_anat" in dbg:
                        qS.inc(g.dma_start(
                            out=dbg["d_anat"][j].rearrange("p (t c) -> p t c", c=160),
                            in_=anat[:]), 16)
                anat_qs.append(qS.n)

            if "d_xnt" in dbg:
                qS_an_last = qS.n

                @block.gpsimd
                def _(g):
                    g.wait_ge(qS.h, qS_an_last)
                    qS.inc(g.dma_start(out=dbg["d_xnt"], in_=xnT[:]), 16)
                qS_dbg = qS.n

                @block.vector
                def _(v):
                    v.wait_ge(qS.h, qS_dbg)

        if upto < 2:
            @block.gpsimd
            def _(g):
                qS.wait(g)
                qL.wait(g)
            return nc
        ctxA = ExitStack()
        A_hT = ctxA.enter_context(nc.sbuf_tensor("A_hT", [128, NP, S], BF16))

        @block.sync
        def _(s):
            s.wait_ge(qS.h, anat_qs[-1])
            for j in range(NP):
                qL.inc(s.dma_start(out=A_hT[:, j, :], in_=anatP_h[j],
                                   transpose=True), 16)
        qL_aht = qL.n
        if "d_aht" in dbg:
            @block.gpsimd
            def _(g):
                g.wait_ge(qL.h, qL_aht)
                qS.inc(g.dma_start(out=dbg["d_aht"], in_=A_hT[:]), 16)

        ctx2 = ExitStack()
        with ctx2:
            s2 = lambda name, shape, dt=BF16: ctx2.enter_context(
                nc.sbuf_tensor(name, list(shape), dt))
            abuf = s2("abuf", (128, 2, 32, 160))
            kaT = s2("kaT", (128, S))
            qaT = s2("qaT", (128, 4, 512))
            wqa_b = s2("wqa_b", (128, 2, 8, 128))
            wka_b = s2("wka_b", (128, 2, 8, 128))
            outT = s2("outT", (128, NP, 2048))
            PT2 = s2("PT2", (128, 4, 512))
            bc_sb = s2("bc_sb", (128, 512), F32)
            rrow = s2("rrow", (1, 1024), F32)
            lof = s2("lof", (1, 1024), F32)
            hirow = s2("hirow", (1, 1024))
            lorow = s2("lorow", (1, 1024))

            wqa_src = d["wqaT"].rearrange("(c p) f -> p c f", p=128)
            wka_src = d["wkaT"].rearrange("(c p) f -> p c f", p=128)

            pv_after_pair = []
            kaqa_end = []

            for j in range(NP):
                jj = j % 2
                qG0 = qG.n

                @block.gpsimd
                def _(g, j=j, jj=jj):
                    if j >= 2:
                        g.wait_ge(tPS.h, kaqa_end[j - 2])
                    qG.inc(g.dma_start(out=wqa_b[:, jj, :, :],
                                       in_=wqa_src[:, :, 128 * j:128 * j + 128]), 16)
                    qG.inc(g.dma_start(out=wka_b[:, jj, :, :],
                                       in_=wka_src[:, :, 128 * j:128 * j + 128]), 16)

                @block.sync
                def _(s, j=j, jj=jj, qs=anat_qs[j]):
                    s.wait_ge(qS.h, qs)
                    if j >= 2:
                        s.wait_ge(tPV.h, pv_after_pair[j - 2])
                    qL.inc(s.dma_start(out=abuf[:, jj, :, :],
                                       in_=anat_h[j].rearrange("p (t c) -> p t c", c=160)), 16)
                qL_a = qL.n

                sEV0, sEX0, vMS0, vNM0, vHL0, tPV0p = (sEV.n, sEX.n, vMS.n,
                                                       vNM.n, vHL.n, tPV.n)
                rec = {}

                def emit_tensor(t, pr, j=j, jj=jj, qG0=qG0, qL_a=qL_a, sEV0=sEV0,
                                sEX0=sEX0, vMS0=vMS0, vNM0=vNM0, vHL0=vHL0):
                    R = rec.setdefault(pr, dict(ka=[], qa=[], sc=[], pv=[], bc=[]))
                    stripes = STRIPES[pr]
                    t.wait_ge(qG.h, qG0 + 32)
                    t.wait_ge(qL.h, qL_a)
                    if j == 0:
                        t.wait_ge(qL.h, qL_aht)
                    t.wait_ge(sEV.h, sEV0)
                    t.wait_ge(sEX.h, sEX0)
                    t.wait_ge(vNM.h, vNM0)
                    for blk in range(8):
                        if blk >= 2:
                            t.wait_ge(sEV.h, sEV0 + blk - 1)
                        ps = pb[6 + blk % 2]
                        for dc in range(8):
                            t.matmul(ps[:, 0:512], wka_b[:, jj, dc, :],
                                     A_hT[:, dc, 512 * blk:512 * blk + 512],
                                     start=(dc == 0), stop=(dc == 7))
                        tPS.inc(t.drain())
                        R["ka"].append(tPS.n)
                    for sl in range(4):
                        g = stripes[sl]
                        t.wait_ge(sEV.h, sEV0 + 8 + (sl - 1 if sl >= 2 else 0))
                        ps = pb[6 + sl % 2]
                        for dc in range(8):
                            t.matmul(ps[:, 0:512], wqa_b[:, jj, dc, :],
                                     A_hT[:, dc, 512 * g:512 * g + 512],
                                     start=(dc == 0), stop=(dc == 7))
                        tPS.inc(t.drain())
                        R["qa"].append(tPS.n)
                    n = 0
                    nd = 0
                    for sl in range(4):
                        g = stripes[sl]
                        nch = 4 * g + 4
                        t.wait_ge(vNM.h, vNM0 + 2 * sl)
                        for c in range(nch):
                            if n >= 2:
                                t.wait_ge(sEX.h, sEX0 + 2 * (n - 2) + 2)
                            for head in range(2):
                                bank = pb[(n % 2) * 2 + head]
                                t.matmul(bank[:, 0:512],
                                         kaT[64 * head:64 * head + 64, 128 * c:128 * c + 128],
                                         qaT[64 * head:64 * head + 64, sl, :],
                                         start=True, stop=True,
                                         tile_position=(64 * head, 0))
                            tSC.inc(t.drain())
                            R["sc"].append(tSC.n)
                            R["sc"].append(tSC.n)
                            is_diag = c >= 4 * g
                            for head in range(2):
                                if is_diag:
                                    nd += 1
                                    t.wait_ge(vMS.h, vMS0 + nd)
                                else:
                                    t.wait_ge(sEX.h, sEX0 + 2 * n + head + 1)
                            for head in range(2):
                                t.matmul(pb[4 + head][0:65, :],
                                         abuf[:, jj, c, 80 * head:80 * head + 65],
                                         PT2[:, (n % 2) * 2 + head, :],
                                         start=(c == 0), stop=(c == nch - 1))
                            tPV.inc(t.drain())
                            R["pv"].append(tPV.n)
                            R["pv"].append(tPV.n)
                            n += 1
                        for head in range(2):
                            t.wait_ge(vHL.h, vHL0 + 2 * sl + head + 1)
                            if sl >= 1:
                                t.wait_ge(sEV.h, sEV0 + 12 + 2 * sl)
                            ps = pb[6 + head]
                            t.matmul(ps[0:64, 0:512], o64[:], hirow[:, 512 * head:512 * head + 512],
                                     start=True, stop=False)
                            t.matmul(ps[0:64, 0:512], o64[:], lorow[:, 512 * head:512 * head + 512],
                                     start=False, stop=True)
                            tBC.inc(t.drain())
                            R["bc"].append(tBC.n)

                def emit_scalar(sc, pr, j=j, tPV0p=tPV0p):
                    R = rec[pr]
                    stripes = STRIPES[pr]
                    for blk in range(8):
                        sc.wait_ge(tPS.h, R["ka"][blk])
                        ps = pb[6 + blk % 2]
                        sEV.inc(sc.activation(kaT[:, 512 * blk:512 * blk + 512],
                                              ps[:, 0:512], AF.Identity,
                                              bias=bka_sb[:, j:j + 1]))
                    for sl in range(4):
                        sc.wait_ge(tPS.h, R["qa"][sl])
                        ps = pb[6 + sl % 2]
                        sEV.inc(sc.activation(qaT[:, sl, :], ps[:, 0:512],
                                              AF.Identity, bias=bqa_sb[:, j:j + 1]))
                    n = 0
                    for sl in range(4):
                        g = stripes[sl]
                        nch = 4 * g + 4
                        for c in range(nch):
                            for head in range(2):
                                sc.wait_ge(tSC.h, R["sc"][2 * n + head])
                                sc.wait_ge(tPV.h, tPV0p if n < 2
                                           else R["pv"][2 * (n - 2) + head])
                                bank = pb[(n % 2) * 2 + head]
                                sEX.inc(sc.activation(PT2[:, (n % 2) * 2 + head, :],
                                                      bank[:, 0:512], AF.Exp, scale=SCALE))
                            n += 1
                        for head in range(2):
                            sc.wait_ge(tBC.h, R["bc"][2 * sl + head])
                            ps = pb[6 + head]
                            sEV.inc(sc.copy(bc_sb[64 * head:64 * head + 64, :],
                                            ps[0:64, 0:512]))

                def emit_vector(v, pr, j=j, sEX0=sEX0, sEV0=sEV0):
                    R = rec[pr]
                    stripes = STRIPES[pr]
                    n = 0
                    for sl in range(4):
                        g = stripes[sl]
                        nch = 4 * g + 4
                        for c in range(nch):
                            if c >= 4 * g:
                                cr = c - 4 * g
                                for head in range(2):
                                    v.wait_ge(sEX.h, sEX0 + 2 * n + head + 1)
                                    vMS.inc(v.tensor_tensor(
                                        PT2[:, (n % 2) * 2 + head, :],
                                        PT2[:, (n % 2) * 2 + head, :],
                                        diag_s[:, cr, :], ALU.mult))
                            n += 1
                        v.wait_ge(tPV.h, R["pv"][2 * n - 1])
                        for head in range(2):
                            v.reciprocal(rrow[:, 512 * head:512 * head + 512], pb[4 + head][64:65, :])
                        for head in range(2):
                            v.tensor_copy(hirow[:, 512 * head:512 * head + 512], rrow[:, 512 * head:512 * head + 512])
                        for head in range(2):
                            v.tensor_tensor(lof[:, 512 * head:512 * head + 512], rrow[:, 512 * head:512 * head + 512],
                                            hirow[:, 512 * head:512 * head + 512], ALU.subtract)
                        for head in range(2):
                            vHL.inc(v.tensor_copy(lorow[:, 512 * head:512 * head + 512],
                                                  lof[:, 512 * head:512 * head + 512]))
                        for head in range(2):
                            v.wait_ge(sEV.h, sEV0 + 12 + 2 * sl + 2)
                            vNM.inc(v.tensor_tensor(
                                outT[64 * head:64 * head + 64, j, 512 * sl:512 * sl + 512],
                                pb[4 + head][0:64, :],
                                bc_sb[64 * head:64 * head + 64, :], ALU.mult))

                def run_branched(decorator, fn):
                    def body(eng):
                        before = {id(c): c.n for c in CTRS}
                        with eng.If_eq(par_reg[eng.engine], 0):
                            fn(eng, 0)
                        after0 = {id(c): c.n for c in CTRS}
                        for c in CTRS:
                            c.n = before[id(c)]
                        with eng.If_eq(par_reg[eng.engine], 1):
                            fn(eng, 1)
                        for c in CTRS:
                            assert c.n == after0[id(c)], "branch sem count mismatch"
                    decorator(body)

                run_branched(block.tensor, emit_tensor)
                run_branched(block.scalar, emit_scalar)
                run_branched(block.vector, emit_vector)
                pv_after_pair.append(tPV.n)
                kaqa_end.append(tPS.n)

            vNM_all = vNM.n

            @block.gpsimd
            def _(g):
                g.wait_ge(vNM.h, vNM_all)
                qS.inc(g.dma_start(out=outT_h, in_=outT[:]), 16)
                if "d_outT" in dbg:
                    qS.inc(g.dma_start(out=dbg["d_outT"], in_=outT[:]), 16)
                if "d_kat" in dbg:
                    qS.inc(g.dma_start(out=dbg["d_kat"], in_=kaT[:]), 16)
                if "d_qat" in dbg:
                    qS.inc(g.dma_start(out=dbg["d_qat"],
                                       in_=qaT[:].rearrange("p a b -> p (a b)")), 16)
            qS_outT = qS.n

            @block.vector
            def _(v):
                v.wait_ge(qS.h, qS_outT)

        ctxA.close()

        if upto < 3:
            @block.gpsimd
            def _(g):
                qS.wait(g)
                qL.wait(g)
            return nc
        ctx3 = ExitStack()
        with ctx3:
            R3_BD, R3_SB2, R3_RB2, R3_RTB = 0, 1024, 2048, 2048 + NE * 1024
            R3L = 2048 + NE * 1024 + NE
            rows3_sb = ctx3.enter_context(nc.sbuf_tensor("rows3_sb", [1, R3L], BF16))

            @block.gpsimd
            def _(g):
                qG.inc(g.dma_start(out=rows3_sb[:, R3_BD:R3_BD + 1024], in_=d["bd_row"]), 16)
                qG.inc(g.dma_start(out=rows3_sb[:, R3_SB2:R3_SB2 + 1024], in_=d["sb2_row"]), 16)
                for e in range(NE):
                    qG.inc(g.dma_start(out=rows3_sb[:, R3_RB2 + 1024 * e:R3_RB2 + 1024 * e + 1024],
                                       in_=d["rb2"][e:e + 1, :]), 16)
                qG.inc(g.dma_start(out=rows3_sb[:, R3_RTB:R3_RTB + NE], in_=d["router_b_row"]), 16)
            qG_rows3 = qG.n

            y_acc = ctx3.enter_context(nc.sbuf_tensor("y_acc", [128, 8, DM], F32))
            xnT2 = ctx3.enter_context(nc.sbuf_tensor("xnT2", [128, 8, 1024], BF16))
            xm_b = ctx3.enter_context(nc.sbuf_tensor("xm_b", [128, 2, DM], F32))
            rprob = ctx3.enter_context(nc.sbuf_tensor("rprob", [128, 8, NE], F32))
            rden = ctx3.enter_context(nc.sbuf_tensor("rden", [128, 8], F32))
            rwork = ctx3.enter_context(nc.sbuf_tensor("rwork", [128, 8, NE], F32))
            rm1 = ctx3.enter_context(nc.sbuf_tensor("rm1", [128, 8], F32))
            rm2 = ctx3.enter_context(nc.sbuf_tensor("rm2", [128, 8], F32))
            xm_src = d["x_mine"].rearrange("(t p) m -> p t m", p=128)
            out_dst = out_h.rearrange("(t p) m -> p t m", p=128)
            xn2_dst = xn2_h.rearrange("(t p) m -> p t m", p=128)

            for half in range(2):
                qS_prev_out = qS.n
                with ExitStack() as ctxWd:
                    outTh = ctxWd.enter_context(nc.sbuf_tensor(f"outTh{half}", [128, 8, 1024], BF16))
                    wdT_b = ctxWd.enter_context(nc.sbuf_tensor(f"wdT_b{half}", [128, 8, 1024], BF16))
                    qL0, qG0, vEV0 = qL.n, qG.n, vEV.n

                    @block.sync
                    def _(s, half=half, vEV0=vEV0):
                        s.wait_ge(qS.h, qS_outT)
                        vEV.wait(s)
                        sEV.wait(s)
                        qL.inc(s.dma_start(out=outTh[:],
                                           in_=outT_h[:, :, 1024 * half:1024 * half + 1024]), 16)
                        for tt in range(8):
                            if tt >= 2:
                                s.wait_ge(vEV.h, vEV0 + 2 * (tt - 2) + 2)
                            qL.inc(s.dma_start(out=xm_b[:, tt % 2, :],
                                               in_=xm_src[:, 8 * half + tt, :]), 16)

                    @block.gpsimd
                    def _(g):
                        qG.inc(g.dma_start(out=wdT_b[:],
                                           in_=d["wdT"].rearrange("(c p) f -> p c f", p=128)), 16)

                    wd_targets = []

                    @block.tensor
                    def _(t, qL0=qL0, qG0=qG0, vEV0=vEV0):
                        t.wait_ge(qL.h, qL0 + 16)
                        t.wait_ge(qG.h, qG0 + 16)
                        t.wait_ge(qG.h, qG_rows3)
                        sEX.wait(t)
                        vNM.wait(t)
                        for gi in range(16):
                            tt, dh = gi // 2, gi % 2
                            if gi >= 2:
                                t.wait_ge(vEV.h, vEV0 + gi - 1)
                            ps = pb[gi % 2]
                            for fc in range(8):
                                t.matmul(ps[:, 0:512], outTh[:, fc, 128 * tt:128 * tt + 128],
                                         wdT_b[:, fc, 512 * dh:512 * dh + 512],
                                         start=(fc == 0), stop=False)
                            t.matmul(ps[:, 0:512], o128[:],
                                     rows3_sb[:, R3_BD + 512 * dh:R3_BD + 512 * dh + 512],
                                     start=False, stop=True)
                            tPS.inc(t.drain())
                            wd_targets.append(tPS.n)

                    @block.vector
                    def _(v, qL0=qL0, qS_prev_out=qS_prev_out):
                        v.wait_ge(qS.h, qS_prev_out)
                        for gi in range(16):
                            tt, dh = gi // 2, gi % 2
                            v.wait_ge(tPS.h, wd_targets[gi])
                            v.wait_ge(qL.h, qL0 + 16 * (tt + 2))
                            ps = pb[gi % 2]
                            vEV.inc(v.tensor_tensor(y_acc[:, tt, 512 * dh:512 * dh + 512],
                                                    ps[:, 0:512],
                                                    xm_b[:, tt % 2, 512 * dh:512 * dh + 512],
                                                    ALU.add))

                with ExitStack() as ctxL2:
                    c2 = lambda name, shape, dt: ctxL2.enter_context(
                        nc.sbuf_tensor(name + str(half), list(shape), dt))
                    xnb2 = c2("xnb2", (128, 4, DM), BF16)
                    trash2 = c2("trash2", (128, 4, DM), BF16)
                    mcol2 = c2("mcol2", (128, 4), F32)
                    msqc2 = c2("msqc2", (128, 4), F32)
                    ssqc2 = c2("ssqc2", (128, 4), F32)
                    varc2 = c2("varc2", (128, 4), F32)
                    sqvc2 = c2("sqvc2", (128, 4), F32)
                    rstdc2 = c2("rstdc2", (128, 4), F32)
                    vEV_wd = vEV.n
                    qS_l2 = qS.n

                    for gidx in range(2):
                        tiles = [4 * gidx + i for i in range(4)]
                        sLN0, vMS0 = sLN.n, vMS.n

                        @block.scalar
                        def _(sc, tiles=tiles, vEV_wd=vEV_wd):
                            sc.wait_ge(vEV.h, vEV_wd)
                            for i, t in enumerate(tiles):
                                sLN.inc(sc.activation(trash2[:, i, :], y_acc[:, t, :], AF.Square,
                                                      accum_out=ssqc2[:, i:i + 1]))

                        @block.vector
                        def _(v, tiles=tiles, sLN0=sLN0, vEV_wd=vEV_wd):
                            v.wait_ge(vEV.h, vEV_wd)
                            for i, t in enumerate(tiles):
                                v.reduce_sum(mcol2[:, i:i + 1], y_acc[:, t, :], axis=AX.X)
                            for i in range(4):
                                v.tensor_scalar_mul(mcol2[:, i:i + 1], mcol2[:, i:i + 1], 1.0 / DM)
                            for i in range(4):
                                v.tensor_tensor(msqc2[:, i:i + 1], mcol2[:, i:i + 1],
                                                mcol2[:, i:i + 1], ALU.mult)
                            v.wait_ge(sLN.h, sLN0 + 4)
                            for i in range(4):
                                v.scalar_tensor_tensor(varc2[:, i:i + 1], ssqc2[:, i:i + 1],
                                                       1.0 / DM, msqc2[:, i:i + 1],
                                                       ALU.mult, ALU.subtract)
                            for i in range(4):
                                vMS.inc(v.tensor_scalar_add(varc2[:, i:i + 1],
                                                            varc2[:, i:i + 1], 1e-5))

                        @block.scalar
                        def _(sc, vMS0=vMS0):
                            sc.wait_ge(vMS.h, vMS0 + 4)
                            for i in range(4):
                                sLN.inc(sc.activation(sqvc2[:, i:i + 1], varc2[:, i:i + 1],
                                                      AF.Sqrt))

                        @block.vector
                        def _(v, tiles=tiles, sLN0=sLN0, gidx=gidx, qS_l2=qS_l2):
                            v.wait_ge(sLN.h, sLN0 + 8)
                            if gidx == 1:
                                v.wait_ge(qS.h, qS_l2 + 16 * 4)
                            for i in range(4):
                                v.reciprocal(rstdc2[:, i:i + 1], sqvc2[:, i:i + 1])
                            for i, t in enumerate(tiles):
                                vEV.inc(v.tensor_scalar(xnb2[:, t % 4, :], y_acc[:, t, :],
                                                        mcol2[:, i:i + 1], rstdc2[:, i:i + 1],
                                                        ALU.subtract, ALU.mult))

                        vEVg = vEV.n

                        @block.gpsimd
                        def _(g, tiles=tiles, vEVg=vEVg, half=half):
                            for i, t in enumerate(tiles):
                                g.wait_ge(vEV.h, vEVg - 3 + i)
                                qS.inc(g.dma_start(out=xn2_dst[:, 8 * half + t, :],
                                                   in_=xnb2[:, t % 4, :]), 16)

                    qS_x2 = qS.n

                    @block.sync
                    def _(s, half=half):
                        s.wait_ge(qS.h, qS_x2)
                        for f in range(8):
                            qL.inc(s.dma_start(out=xnT2[:, f, :],
                                               in_=xn2_h[1024 * half:1024 * half + 1024,
                                                         128 * f:128 * f + 128],
                                               transpose=True), 16)
                qL_x2t = qL.n

                rt_targets = []
                sEX0r = sEX.n

                @block.tensor
                def _(t):
                    t.wait_ge(qL.h, qL_x2t)
                    t.wait_ge(qG.h, qG_P0)
                    for tt in range(8):
                        if tt >= 2:
                            t.wait_ge(sEX.h, sEX0r + tt - 1)
                        ps = pb[2 + tt % 2]
                        for dc in range(8):
                            t.matmul(ps[:, 0:NE], xnT2[:, dc, 128 * tt:128 * tt + 128],
                                     rwr_sb[:, dc, :], start=(dc == 0), stop=False)
                        t.matmul(ps[:, 0:NE], o128[:], rows3_sb[:, R3_RTB:R3_RTB + NE],
                                 start=False, stop=True)
                        tPS.inc(t.drain())
                        rt_targets.append(tPS.n)

                @block.scalar
                def _(sc):
                    for tt in range(8):
                        sc.wait_ge(tPS.h, rt_targets[tt])
                        ps = pb[2 + tt % 2]
                        sEX.inc(sc.activation(rprob[:, tt, :], ps[:, 0:NE], AF.Exp,
                                              accum_out=rden[:, tt:tt + 1]))
                sEX_rt = sEX.n

                @block.vector
                def _(v):
                    v.wait_ge(sEX.h, sEX_rt)
                    vIN.wait(v)
                    for tt in range(8):
                        v.reciprocal(rm1[:, tt:tt + 1], rden[:, tt:tt + 1])
                    for tt in range(8):
                        v.tensor_scalar(rprob[:, tt, :], rprob[:, tt, :],
                                        rm1[:, tt:tt + 1], None, ALU.mult)
                    for tt in range(8):
                        v.reduce_max(rm1[:, tt:tt + 1], rprob[:, tt, :], axis=AX.X)
                    for tt in range(8):
                        v.tensor_scalar(rwork[:, tt, :], rprob[:, tt, :],
                                        rm1[:, tt:tt + 1], None, ALU.is_ge)
                    for tt in range(8):
                        v.scalar_tensor_tensor(rwork[:, tt, :], rwork[:, tt, :], -1e9,
                                               rprob[:, tt, :], ALU.mult, ALU.add)
                    for tt in range(8):
                        v.reduce_max(rm2[:, tt:tt + 1], rwork[:, tt, :], axis=AX.X)
                    for tt in range(8):
                        v.tensor_scalar(rwork[:, tt, :], rprob[:, tt, :],
                                        rm2[:, tt:tt + 1], None, ALU.is_ge)
                    for tt in range(8):
                        vIN.inc(v.tensor_tensor(gates_sb[:, tt, 0:NE], rprob[:, tt, :],
                                                rwork[:, tt, :], ALU.mult))
                vIN_gt = vIN.n

                if "d_gates" in dbg and half == 0:
                    @block.gpsimd
                    def _(g, vIN_gt=vIN_gt):
                        g.wait_ge(vIN.h, vIN_gt)
                        qS.inc(g.dma_start(out=dbg["d_gates"], in_=gates_sb[:]), 16)

                with ExitStack() as ctxEx:
                    w1b = ctxEx.enter_context(nc.sbuf_tensor(f"w1b{half}", [128, 2, 8, 1024], BF16))
                    w2b = ctxEx.enter_context(nc.sbuf_tensor(f"w2b{half}", [128, 2, 8, 1024], BF16))
                    hT = ctxEx.enter_context(nc.sbuf_tensor(f"hT{half}", [128, 8, 1024], BF16))

                    mm2_after_step = []
                    for step in range(18):
                        e, hh = step // 2, step % 2
                        ss = step % 2

                        @block.gpsimd
                        def _(g, e=e, hh=hh, ss=ss, step=step):
                            if step >= 2:
                                g.wait_ge(tPS.h, mm2_after_step[step - 2])
                            if e < NE:
                                w1s = d["rw1"][e].rearrange("(c p) f -> p c f", p=128)[:, :, 1024 * hh:1024 * hh + 1024]
                                w2s = d["rw2"][e][1024 * hh:1024 * hh + 1024].rearrange("(c p) f -> p c f", p=128)
                            else:
                                w1s = d["sw1"].rearrange("(c p) f -> p c f", p=128)[:, :, 1024 * hh:1024 * hh + 1024]
                                w2s = d["sw2"][1024 * hh:1024 * hh + 1024].rearrange("(c p) f -> p c f", p=128)
                            qG.inc(g.dma_start(out=w1b[:, ss, :, :], in_=w1s), 16)
                            qG.inc(g.dma_start(out=w2b[:, ss, :, :], in_=w2s), 16)

                        gl_targets = []
                        m2_targets = []
                        sEV0e, vEV0e = sEV.n, vEV.n
                        qG1 = qG.n

                        @block.tensor
                        def _(t, e=e, hh=hh, ss=ss, qG1=qG1, sEV0e=sEV0e, vEV0e=vEV0e):
                            t.wait_ge(qG.h, qG1)
                            sEX.wait(t)
                            for gi in range(16):
                                ht, b2 = gi // 2, gi % 2
                                t.wait_ge(sEV.h, sEV0e + (gi - 1 if gi >= 2 else 0))
                                ps = pb[gi % 2]
                                for dc in range(8):
                                    t.matmul(ps[:, 0:512],
                                             w1b[:, ss, dc, 128 * ht:128 * ht + 128],
                                             xnT2[:, dc, 512 * b2:512 * b2 + 512],
                                             start=(dc == 0), stop=(dc == 7))
                                tPS.inc(t.drain())
                                gl_targets.append(tPS.n)
                            t.wait_ge(sEV.h, sEV0e + 16)
                            for gi in range(16):
                                tt, dh = gi // 2, gi % 2
                                t.wait_ge(vEV.h, vEV0e + (gi - 1 if gi >= 2 else 0))
                                ps = pb[2 + gi % 2]
                                for hc in range(8):
                                    t.matmul(ps[:, 0:512], hT[:, hc, 128 * tt:128 * tt + 128],
                                             w2b[:, ss, hc, 512 * dh:512 * dh + 512],
                                             start=(hc == 0), stop=(hh == 0 and hc == 7))
                                if hh == 1:
                                    if e < NE:
                                        brow = rows3_sb[:, R3_RB2 + 1024 * e + 512 * dh:
                                                        R3_RB2 + 1024 * e + 512 * dh + 512]
                                    else:
                                        brow = rows3_sb[:, R3_SB2 + 512 * dh:R3_SB2 + 512 * dh + 512]
                                    t.matmul(ps[:, 0:512], o128[:], brow,
                                             start=False, stop=True)
                                tPS.inc(t.drain())
                                m2_targets.append(tPS.n)
                        mm2_after_step.append(tPS.n)

                        @block.scalar
                        def _(sc, e=e, hh=hh):
                            for gi in range(16):
                                ht, b2 = gi // 2, gi % 2
                                sc.wait_ge(tPS.h, gl_targets[gi])
                                ps = pb[gi % 2]
                                if e < NE:
                                    bias = rb1_sb[:, e, 8 * hh + ht:8 * hh + ht + 1]
                                else:
                                    bias = sb1_sb[:, 8 * hh + ht:8 * hh + ht + 1]
                                sEV.inc(sc.activation(hT[:, ht, 512 * b2:512 * b2 + 512],
                                                      ps[:, 0:512], AF.Gelu, bias=bias))

                        @block.vector
                        def _(v, e=e, vIN_gt=vIN_gt):
                            v.wait_ge(vIN.h, vIN_gt)
                            gcol = min(e, 8)
                            for gi in range(16):
                                tt, dh = gi // 2, gi % 2
                                v.wait_ge(tPS.h, m2_targets[gi])
                                ps = pb[2 + gi % 2]
                                vEV.inc(v.scalar_tensor_tensor(
                                    y_acc[:, tt, 512 * dh:512 * dh + 512], ps[:, 0:512],
                                    gates_sb[:, tt, gcol:gcol + 1],
                                    y_acc[:, tt, 512 * dh:512 * dh + 512],
                                    ALU.mult, ALU.add))

                vEV_fin = vEV.n

                @block.gpsimd
                def _(g, half=half, vEV_fin=vEV_fin):
                    g.wait_ge(vEV.h, vEV_fin)
                    qS.inc(g.dma_start(out=out_dst[:, 8 * half:8 * half + 8, :],
                                       in_=y_acc[:]), 16)
                    if "d_yacc" in dbg and half == 0:
                        qS.inc(g.dma_start(out=dbg["d_yacc"], in_=y_acc[:]), 16)

        @block.gpsimd
        def _(g):
            qS.wait(g)
            qL.wait(g)

    return nc


def _in_maps(inputs):
    x = np.ascontiguousarray(inputs["L0_attn_hidden_states"], dtype=np.float32)
    w = {k: np.ascontiguousarray(v, dtype=np.float32) for k, v in inputs.items()}
    shared = {
        "wqkvT": np.ascontiguousarray(w["L0_attn_Wqkv"].T),
        "bqkv_pc": np.ascontiguousarray(w["L0_attn_bqkv"].reshape(24, 128).T),
        "bqkv_row": w["L0_attn_bqkv"].reshape(1, -1),
        "wqaT": np.ascontiguousarray(w["L0_attn_Wqa"].T),
        "bqa_pc": np.ascontiguousarray(w["L0_attn_bqa"].reshape(8, 128).T),
        "wkaT": np.ascontiguousarray(w["L0_attn_Wka"].T),
        "bka_pc": np.ascontiguousarray(w["L0_attn_bka"].reshape(8, 128).T),
        "wdT": np.ascontiguousarray(w["L0_attn_Wd"].T),
        "bd_row": w["L0_attn_bd"].reshape(1, -1),
        "router_w_pc": np.ascontiguousarray(w["L0_ffn_router_w"].reshape(8, 128, NE).transpose(1, 0, 2)),
        "router_b_row": w["L0_ffn_router_b"].reshape(1, -1),
        "sw1": w["L0_ffn_sw1"],
        "sb1_pc": np.ascontiguousarray(w["L0_ffn_sb1"].reshape(16, 128).T),
        "sw2": w["L0_ffn_sw2"],
        "sb2_row": w["L0_ffn_sb2"].reshape(1, -1),
        "rw1": w["L0_ffn_rw1"],
        "rb1_pc": np.ascontiguousarray(w["L0_ffn_rb1"].reshape(NE, 16, 128).transpose(2, 0, 1)),
        "rw2": w["L0_ffn_rw2"],
        "rb2": w["L0_ffn_rb2"],
    }
    maps = []
    for c in range(8):
        b, p = c // 2, c % 2
        stripes = STRIPES[p]
        xm = np.ascontiguousarray(x[b].reshape(8, 512, DM)[stripes].reshape(2048, DM))
        m = dict(shared)
        m["x_b"] = x[b]
        m["x_mine"] = xm
        maps.append(m)
    return maps


def kernel(**inputs):
    nc = build_program(debug=_DEBUG)
    maps = _in_maps(inputs)
    res = run_bass_kernel_spmd(nc, maps, core_ids=list(range(8)))
    y = np.zeros((4, S, DM), dtype=np.float32)
    for c in range(8):
        b, p = c // 2, c % 2
        shard = np.asarray(res.results[c]["out"]).reshape(4, 512, DM)
        for i, g in enumerate(STRIPES[p]):
            y[b, 512 * g:512 * g + 512] = shard[i]
    return y


# version 07 (final)
# speedup vs baseline: 1.0242x; 1.0242x over previous
from contextlib import ExitStack
import numpy as np
import ml_dtypes
import concourse.bass as bass
import concourse.mybir as mybir
from concourse.bass_utils import run_bass_kernel_spmd

F32 = mybir.dt.float32
BF16 = mybir.dt.bfloat16
AF = mybir.ActivationFunctionType
ALU = mybir.AluOpType
AX = mybir.AxisListType

S = 4096
DM = 1024
NP = 8
W = 16
SCALE = 0.125
FF = 2048
NE = 8
STRIPES = ([0, 3, 5, 6], [1, 2, 4, 7])
BF = ml_dtypes.bfloat16

_DEBUG = ()


def _rope_tables():
    inv = 1.0 / 10000.0 ** (np.arange(0, 16, 2, dtype=np.float64) / 16.0)
    t = np.arange(S, dtype=np.float64)
    fr = np.outer(inv, t)
    C = np.ones((128, S), dtype=np.float64)
    Sn = np.zeros((128, S), dtype=np.float64)
    for base in (0, 64):
        C[base:base + 8] = np.cos(fr)
        C[base + 8:base + 16] = np.cos(fr)
        Sn[base:base + 8] = -np.sin(fr)
        Sn[base + 8:base + 16] = np.sin(fr)
    return C.astype(np.float32).astype(BF), Sn.astype(np.float32).astype(BF)


def _win_masks():
    kk = np.arange(128)[:, None]
    qq = np.arange(128)[None, :]
    m_same = ((qq - kk >= 0) & (qq - kk <= W - 1)).astype(np.float32)
    m_prev = ((qq + 128 - kk >= 0) & (qq + 128 - kk <= W - 1)).astype(np.float32)
    return m_same.astype(BF), m_prev.astype(BF)


def _diag_masks():
    kk = np.arange(128)[:, None]
    qq = np.arange(512)[None, :]
    out = np.zeros((4, 128, 512), dtype=np.float32)
    for cr in range(4):
        out[cr] = (128 * cr + kk <= qq).astype(np.float32)
    return out.astype(BF)


ROT_MASK = [(i + 8) % 16 if (i % 32) < 16 else i for i in range(32)]


class Ctr:
    def __init__(self, ctx, nc, name):
        self.h = ctx.enter_context(nc.semaphore(name))
        self.n = 0

    def inc(self, instr, k=1):
        instr.then_inc(self.h, k)
        self.n += k
        return instr

    def wait(self, eng, val=None):
        eng.wait_ge(self.h, self.n if val is None else val)


def build_program(debug=(), upto=3, p1_pairs=NP, p1_win=True, p1_rope=True, p1_v=True, p1_qk=True):
    nc = bass.Bass("TRN2", target_bir_lowering=False, detect_race_conditions=False)

    d = {}
    def din(name, shape):
        d[name] = nc.dram_tensor(name, list(shape), F32, kind="ExternalInput").ap()
    din("x_b", (S, DM))
    din("x_mine", (2048, DM))
    din("wqkvT", (DM, 3 * DM))
    din("bqkv_pc", (128, 24))
    din("bqkv_row", (1, 3 * DM))
    din("wqaT", (DM, DM))
    din("bqa_pc", (128, 8))
    din("wkaT", (DM, DM))
    din("bka_pc", (128, 8))
    din("wdT", (DM, DM))
    din("bd_row", (1, DM))
    din("router_w_pc", (128, 8, NE))
    din("router_b_row", (1, NE))
    din("sw1", (DM, FF))
    din("sb1_pc", (128, 16))
    din("sw2", (FF, DM))
    din("sb2_row", (1, DM))
    din("rw1", (NE, DM, FF))
    din("rb1_pc", (128, NE, 16))
    din("rw2", (NE, FF, DM))
    din("rb2", (NE, DM))
    out_h = nc.dram_tensor("out", [2048, DM], F32, kind="ExternalOutput").ap()

    dbg = {}
    for nm in debug:
        shp = {"d_anat": [NP, 128, 32 * 160], "d_aht": [128, NP, S],
               "d_xnt": [128, NP, S], "d_outT": [128, NP, 2048],
               "d_kat": [128, S], "d_qat": [128, 4 * 512],
               "d_gates": [128, 8, 9], "d_yacc": [128, 8, DM]}[nm]
        dbg[nm] = nc.dram_tensor(nm, shp, F32, kind="ExternalOutput").ap()

    xn1_h = nc.dram_tensor("xn1_h", [S, DM], BF16, kind="Internal").ap()
    anat_h = nc.dram_tensor("anat_h", [NP, 128, 32 * 160], BF16, kind="Internal").ap()
    anatP_h = nc.dram_tensor("anatP_h", [NP, S, 128], BF16, kind="Internal").ap()
    outT_h = nc.dram_tensor("outT_h", [128, NP, 2048], BF16, kind="Internal").ap()
    xn2_h = nc.dram_tensor("xn2_h", [2048, DM], BF16, kind="Internal").ap()

    COSt, SINt = _rope_tables()
    MSAME, MPREV = _win_masks()
    DIAG = _diag_masks()
    cos_t = nc.inline_tensor(np.ascontiguousarray(COSt), name="cos_t").ap()
    sin_t = nc.inline_tensor(np.ascontiguousarray(SINt), name="sin_t").ap()
    msame_t = nc.inline_tensor(np.ascontiguousarray(MSAME), name="msame_t").ap()
    mprev_t = nc.inline_tensor(np.ascontiguousarray(MPREV), name="mprev_t").ap()
    diag_t = nc.inline_tensor(np.ascontiguousarray(DIAG), name="diag_t").ap()
    ones128_t = nc.inline_tensor(np.ones((1, 128), dtype=BF), name="ones128_t").ap()
    ones64_t = nc.inline_tensor(np.ones((1, 64), dtype=BF), name="ones64_t").ap()

    L0 = ExitStack()
    with L0:
        sbuf = lambda name, shape, dt=BF16: L0.enter_context(
            nc.sbuf_tensor(name, list(shape), dt))
        cos_s = sbuf("cos_s", (128, S))
        sin_s = sbuf("sin_s", (128, S))
        msame_s = sbuf("msame_s", (128, 128))
        mprev_s = sbuf("mprev_s", (128, 128))
        diag_s = sbuf("diag_s", (128, 4, 512))
        o128 = sbuf("o128", (1, 128))
        o64 = sbuf("o64", (1, 64))
        bq_sb = sbuf("bq_sb", (128, 24), F32)
        bqa_sb = sbuf("bqa_sb", (128, 8), F32)
        bka_sb = sbuf("bka_sb", (128, 8), F32)
        sb1_sb = sbuf("sb1_sb", (128, 16), F32)
        rb1_sb = sbuf("rb1_sb", (128, NE, 16), F32)
        rwr_sb = sbuf("rwr_sb", (128, 8, NE))
        gates_sb = sbuf("gates_sb", (128, 8, 9), F32)

        pb = [L0.enter_context(nc.psum_tensor(f"pb{i}", [128, 512], F32))
              for i in range(8)]

        qL = Ctr(L0, nc, "qL")
        qG = Ctr(L0, nc, "qG")
        qS = Ctr(L0, nc, "qS")
        tPS = Ctr(L0, nc, "tPS")
        tSC = Ctr(L0, nc, "tSC")
        tPV = Ctr(L0, nc, "tPV")
        tBC = Ctr(L0, nc, "tBC")
        sEV = Ctr(L0, nc, "sEV")
        sEX = Ctr(L0, nc, "sEX")
        sLN = Ctr(L0, nc, "sLN")
        vIN = Ctr(L0, nc, "vIN")
        vRP = Ctr(L0, nc, "vRP")
        vMS = Ctr(L0, nc, "vMS")
        vNM = Ctr(L0, nc, "vNM")
        vHL = Ctr(L0, nc, "vHL")
        vEV = Ctr(L0, nc, "vEV")
        qC = Ctr(L0, nc, "qC")
        CTRS = [qL, qG, qS, tPS, tSC, tPV, tBC, sEV, sEX, sLN,
                vIN, vRP, vMS, vNM, vHL, vEV, qC]

        block = L0.enter_context(nc.Block())
        par_reg = {}

        def setup_parity(eng):
            pid = eng.alloc_register("pid_" + eng.engine.name)
            par = eng.alloc_register("par_" + eng.engine.name)
            eng.reg_mov(pid, eng.partition_id())
            eng.reg_mod(par, pid, 2)
            par_reg[eng.engine] = par

        @block.sync
        def _(s):
            for dst, src in ((cos_s, cos_t), (sin_s, sin_t), (msame_s, msame_t),
                             (mprev_s, mprev_t), (o128, ones128_t), (o64, ones64_t)):
                qL.inc(s.dma_start(out=dst[:], in_=src), 16)
            for cr in range(4):
                qL.inc(s.dma_start(out=diag_s[:, cr, :], in_=diag_t[cr]), 16)
            qL.inc(s.dma_start(out=bq_sb[:], in_=d["bqkv_pc"]), 16)
            qL.inc(s.dma_start(out=bqa_sb[:], in_=d["bqa_pc"]), 16)
            qL.inc(s.dma_start(out=bka_sb[:], in_=d["bka_pc"]), 16)
            qL.inc(s.dma_start(out=sb1_sb[:], in_=d["sb1_pc"]), 16)
            qL.inc(s.dma_start(out=rb1_sb[:], in_=d["rb1_pc"]), 16)
        qL_P0 = qL.n

        @block.gpsimd
        def _(g):
            qG.inc(g.dma_start(out=rwr_sb[:], in_=d["router_w_pc"]), 16)
        qG_P0 = qG.n

        @block.vector
        def _(v):
            setup_parity(v)
            vIN.inc(v.memset(gates_sb[:, :, 8:9], 1.0))

        @block.tensor
        def _(t):
            setup_parity(t)

        @block.scalar
        def _(sc):
            setup_parity(sc)


        qkv_end = []
        anat_qs = []

        ctx1 = ExitStack()
        with ctx1:
            with ExitStack() as ctxLN:
                c1 = lambda name, shape, dt: ctxLN.enter_context(
                    nc.sbuf_tensor(name, list(shape), dt))
                xb = c1("xb", (128, 4, DM), F32)
                xnb = c1("xnb", (128, 4, DM), BF16)
                trash = c1("trash", (128, 4, DM), BF16)
                mcol = c1("mcol", (128, 4), F32)
                msqc = c1("msqc", (128, 4), F32)
                ssqc = c1("ssqc", (128, 4), F32)
                varc = c1("varc", (128, 4), F32)
                sqvc = c1("sqvc", (128, 4), F32)
                rstdc = c1("rstdc", (128, 4), F32)

                x_src = d["x_b"].rearrange("(t p) m -> p t m", p=128)
                xn_dst = xn1_h.rearrange("(t p) m -> p t m", p=128)
                qL0 = qL.n
                vEV0 = vEV.n
                qS0 = qS.n

                @block.sync
                def _(s):
                    for t in range(32):
                        if t >= 4:
                            s.wait_ge(vEV.h, vEV0 + t - 3)
                        qL.inc(s.dma_start(out=xb[:, t % 4, :], in_=x_src[:, t, :]), 16)

                for gidx in range(8):
                    tiles = [4 * gidx + i for i in range(4)]
                    sLN0, vMS0 = sLN.n, vMS.n
                    tl0 = qL0 + 16 * (tiles[-1] + 1)

                    @block.scalar
                    def _(sc, tiles=tiles, tl0=tl0):
                        sc.wait_ge(qL.h, tl0)
                        for i, t in enumerate(tiles):
                            sLN.inc(sc.activation(trash[:, i, :], xb[:, t % 4, :], AF.Square,
                                                  accum_out=ssqc[:, i:i + 1]))

                    @block.vector
                    def _(v, tiles=tiles, tl0=tl0, sLN0=sLN0):
                        v.wait_ge(qL.h, tl0)
                        for i, t in enumerate(tiles):
                            v.reduce_sum(mcol[:, i:i + 1], xb[:, t % 4, :], axis=AX.X)
                        for i in range(4):
                            v.tensor_scalar_mul(mcol[:, i:i + 1], mcol[:, i:i + 1], 1.0 / DM)
                        for i in range(4):
                            v.tensor_tensor(msqc[:, i:i + 1], mcol[:, i:i + 1],
                                            mcol[:, i:i + 1], ALU.mult)
                        v.wait_ge(sLN.h, sLN0 + 4)
                        for i in range(4):
                            v.scalar_tensor_tensor(varc[:, i:i + 1], ssqc[:, i:i + 1],
                                                   1.0 / DM, msqc[:, i:i + 1],
                                                   ALU.mult, ALU.subtract)
                        for i in range(4):
                            vMS.inc(v.tensor_scalar_add(varc[:, i:i + 1], varc[:, i:i + 1], 1e-5))

                    @block.scalar
                    def _(sc, vMS0=vMS0):
                        sc.wait_ge(vMS.h, vMS0 + 4)
                        for i in range(4):
                            sLN.inc(sc.activation(sqvc[:, i:i + 1], varc[:, i:i + 1], AF.Sqrt))

                    @block.vector
                    def _(v, tiles=tiles, sLN0=sLN0, qS0=qS0):
                        v.wait_ge(sLN.h, sLN0 + 8)
                        for i in range(4):
                            v.reciprocal(rstdc[:, i:i + 1], sqvc[:, i:i + 1])
                        for i, t in enumerate(tiles):
                            if t >= 4:
                                v.wait_ge(qS.h, qS0 + 16 * (t - 3))
                            vEV.inc(v.tensor_scalar(xnb[:, t % 4, :], xb[:, t % 4, :],
                                                    mcol[:, i:i + 1], rstdc[:, i:i + 1],
                                                    ALU.subtract, ALU.mult))

                    vEVg = vEV.n

                    @block.gpsimd
                    def _(g, tiles=tiles, vEVg=vEVg):
                        for i, t in enumerate(tiles):
                            g.wait_ge(vEV.h, vEVg - 3 + i)
                            qS.inc(g.dma_start(out=xn_dst[:, t, :], in_=xnb[:, t % 4, :]), 16)

            xnT = ctx1.enter_context(nc.sbuf_tensor("xnT", [128, 8, S], BF16))
            qS_ln = qS.n

            @block.sync
            def _(s):
                s.wait_ge(qS.h, qS_ln)
                for f in range(8):
                    qL.inc(s.dma_start(out=xnT[:, f, :],
                                       in_=xn1_h[:, 128 * f:128 * f + 128],
                                       transpose=True), 16)
            qL_xnT = qL.n

            if upto < 1:
                if "d_xnt" in dbg:
                    @block.gpsimd
                    def _(g):
                        g.wait_ge(qL.h, qL_xnT)
                        qS.inc(g.dma_start(out=dbg["d_xnt"], in_=xnT[:]), 16)

                @block.gpsimd
                def _(g):
                    qS.wait(g)
                    qL.wait(g)
                return nc

            rows1_sb = ctx1.enter_context(nc.sbuf_tensor("rows1_sb", [1, 3072], BF16))

            @block.gpsimd
            def _(g):
                qG.inc(g.dma_start(out=rows1_sb[:], in_=d["bqkv_row"]), 16)

            wq_b = ctx1.enter_context(nc.sbuf_tensor("wq_b", [128, 2, 8, 128], BF16))
            wk_b = ctx1.enter_context(nc.sbuf_tensor("wk_b", [128, 2, 8, 128], BF16))
            wv_b = ctx1.enter_context(nc.sbuf_tensor("wv_b", [128, 2, 8, 128], BF16))
            qTp = ctx1.enter_context(nc.sbuf_tensor("qTp", [128, S], BF16))
            kTp = ctx1.enter_context(nc.sbuf_tensor("kTp", [128, S], BF16))
            vnat = ctx1.enter_context(nc.sbuf_tensor("vnat", [128, 32, 160], BF16))
            anat = ctx1.enter_context(nc.sbuf_tensor("anat", [128, 32, 160], BF16))
            rot_b = ctx1.enter_context(nc.sbuf_tensor("rot_b", [128, 2, 1024], BF16))
            t2f = ctx1.enter_context(nc.sbuf_tensor("t2f", [128, 2, 1024], F32))
            t3f = ctx1.enter_context(nc.sbuf_tensor("t3f", [128, 2, 1024], F32))
            PT = ctx1.enter_context(nc.sbuf_tensor("PT", [128, 4, 128], BF16))
            rcol = ctx1.enter_context(nc.sbuf_tensor("rcol", [128, 2], F32))

            wsrc = d["wqkvT"].rearrange("(c p) f -> p c f", p=128)

            for j in range(p1_pairs):
                jj = j % 2
                qG0 = qG.n

                @block.gpsimd
                def _(g, j=j, jj=jj):
                    if j >= 2:
                        g.wait_ge(tPS.h, qkv_end[j - 2])
                    qG.inc(g.dma_start(out=wq_b[:, jj, :, :],
                                       in_=wsrc[:, :, 128 * j:128 * j + 128]), 16)
                    qG.inc(g.dma_start(out=wk_b[:, jj, :, :],
                                       in_=wsrc[:, :, 1024 + 128 * j:1024 + 128 * j + 128]), 16)
                    qG.inc(g.dma_start(out=wv_b[:, jj, :, :],
                                       in_=wsrc[:, :, 2048 + 128 * j:2048 + 128 * j + 128]), 16)

                qS_prev = qL.n

                @block.vector
                def _(v, qS_prev=qS_prev):
                    v.wait_ge(qL.h, qS_prev)
                    v.memset(vnat[:], 0.0)
                    v.memset(anat[:], 0.0)
                    v.memset(vnat[:, :, 64:65], 1.0)
                    v.memset(vnat[:, :, 144:145], 1.0)
                    v.memset(anat[:, :, 64:65], 1.0)
                    vIN.inc(v.memset(anat[:, :, 144:145], 1.0))
                vIN_j = vIN.n

                sEV0, tPV0 = sEV.n, tPV.n
                q_targets = []

                @block.tensor
                def _(t, j=j, jj=jj, qG0=qG0, sEV0=sEV0):
                    t.wait_ge(qL.h, qL_xnT)
                    t.wait_ge(qG.h, qG0 + 48)
                    vNM.wait(t)
                    for op, wbuf in (((0, wq_b), (1, wk_b)) if p1_qk else ()):
                        for blk in range(8):
                            gi = op * 8 + blk
                            if gi >= 2:
                                t.wait_ge(sEV.h, sEV0 + gi - 1)
                            ps = pb[gi % 2]
                            for dc in range(8):
                                t.matmul(ps[:, 0:512], wbuf[:, jj, dc, :],
                                         xnT[:, dc, 512 * blk:512 * blk + 512],
                                         start=(dc == 0), stop=(dc == 7))
                            tPS.inc(t.drain())
                            q_targets.append(tPS.n)
                    if p1_v:
                        for tt in range(32):
                            gi = 16 + tt
                            t.wait_ge(sEV.h, sEV0 + gi - 1)
                            ps = pb[gi % 2]
                            for dc in range(8):
                                t.matmul(ps[:, 0:128], xnT[:, dc, 128 * tt:128 * tt + 128],
                                         wv_b[:, jj, dc, :], start=(dc == 0), stop=False)
                            t.matmul(ps[:, 0:128], o128[:],
                                     rows1_sb[:, 2048 + 128 * j:2048 + 128 * j + 128],
                                     start=False, stop=True)
                            tPS.inc(t.drain())
                            q_targets.append(tPS.n)
                qkv_end.append(tPS.n)

                @block.scalar
                def _(sc, j=j, vIN_j=vIN_j):
                    it = iter(q_targets)
                    if p1_qk:
                        for op, dst, bcol in ((0, qTp, j), (1, kTp, 8 + j)):
                            for blk in range(8):
                                sc.wait_ge(tPS.h, next(it))
                                ps = pb[(op * 8 + blk) % 2]
                                sEV.inc(sc.activation(dst[:, 512 * blk:512 * blk + 512],
                                                      ps[:, 0:512], AF.Identity,
                                                      bias=bq_sb[:, bcol:bcol + 1]))
                    sc.wait_ge(vIN.h, vIN_j)
                    if p1_v:
                        for tt in range(32):
                            sc.wait_ge(tPS.h, next(it))
                            ps = pb[tt % 2]
                            dst = vnat[:, tt, :].rearrange("p (b c) -> p b c", b=2)[:, :, 0:64]
                            src = ps[:, 0:128].rearrange("p (b c) -> p b c", b=2)
                            sEV.inc(sc.activation(dst, src, AF.Identity))

                sEV_qk = sEV0 + (16 if p1_qk else 0)

                @block.vector
                def _(v, sEV_qk=sEV_qk):
                    if not p1_rope:
                        vRP.inc(v.memset(rcol[:], 0.0))
                        vRP.inc(v.memset(rcol[:], 0.0))
                        return
                    v.wait_ge(sEV.h, sEV_qk)
                    ins = None
                    for dst in (qTp, kTp):
                        for bp in range(2):
                            for hb in range(2):
                                blk = 2 * bp + hb
                                v.stream_shuffle(rot_b[:, hb, :],
                                                 dst[:, 1024 * blk:1024 * blk + 1024], ROT_MASK)
                            for hb in range(2):
                                blk = 2 * bp + hb
                                v.tensor_tensor(t2f[:, hb, :], rot_b[:, hb, :],
                                                sin_s[:, 1024 * blk:1024 * blk + 1024], ALU.mult)
                            for hb in range(2):
                                blk = 2 * bp + hb
                                v.tensor_tensor(t3f[:, hb, :],
                                                dst[:, 1024 * blk:1024 * blk + 1024],
                                                cos_s[:, 1024 * blk:1024 * blk + 1024], ALU.mult)
                            for hb in range(2):
                                blk = 2 * bp + hb
                                ins = v.tensor_tensor(dst[:, 1024 * blk:1024 * blk + 1024],
                                                      t2f[:, hb, :], t3f[:, hb, :], ALU.add)
                        vRP.inc(ins)
                vRP_j = vRP.n

                if not p1_win:
                    continue
                sEX0, vMS0, vNM0 = sEX.n, vMS.n, vNM.n
                pv_targets = []
                sc_targets = []

                @block.tensor
                def _(t, j=j, vRP_j=vRP_j, sEX0=sEX0, vMS0=vMS0, vNM0=vNM0):
                    t.wait_ge(vRP.h, vRP_j)
                    t.wait_ge(vNM.h, vNM0)
                    t.wait_ge(sEX.h, sEX0)
                    n = 0
                    for qc in range(32):
                        kcs = ([qc - 1] if qc > 0 else []) + [qc]
                        qtiles = []
                        for kc in kcs:
                            for head in range(2):
                                if n >= 4:
                                    t.wait_ge(sEX.h, sEX0 + n - 3)
                                bank = pb[2 + n % 4]
                                t.matmul(bank[:, 0:128],
                                         kTp[64 * head:64 * head + 64, 128 * kc:128 * kc + 128],
                                         qTp[64 * head:64 * head + 64, 128 * qc:128 * qc + 128],
                                         start=True, stop=True,
                                         tile_position=(64 * head, 0))
                                qtiles.append((n, kc, head))
                                n += 1
                            tSC.inc(t.drain())
                            sc_targets.append(tSC.n)
                            sc_targets.append(tSC.n)
                        outb = [pb[6], pb[7]] if qc % 2 == 0 else [pb[0], pb[1]]
                        for bidx in range(len(qtiles) // 2):
                            pair = qtiles[2 * bidx:2 * bidx + 2]
                            for tn, kc, head in pair:
                                t.wait_ge(vMS.h, vMS0 + tn + 1)
                            first = bidx == 0
                            last = bidx == len(qtiles) // 2 - 1
                            if first and qc >= 2:
                                t.wait_ge(vNM.h, vNM0 + 2 * qc - 2)
                            for tn, kc, head in pair:
                                t.matmul(outb[head][:, 0:65], PT[:, tn % 4, :],
                                         vnat[:, kc, 80 * head:80 * head + 65],
                                         start=first, stop=last)
                            tPV.inc(t.drain())
                            pv_targets.append(tPV.n)
                            pv_targets.append(tPV.n)
                ntl = len(sc_targets)

                @block.scalar
                def _(sc, tPV0=tPV0):
                    for n in range(ntl):
                        sc.wait_ge(tSC.h, sc_targets[n])
                        sc.wait_ge(tPV.h, tPV0 if n < 4 else pv_targets[n - 4])
                        bank = pb[2 + n % 4]
                        sEX.inc(sc.activation(PT[:, n % 4, :], bank[:, 0:128],
                                              AF.Exp, scale=SCALE))

                @block.vector
                def _(v, j=j, sEX0=sEX0):
                    n = 0
                    for qc in range(32):
                        kcs = ([qc - 1] if qc > 0 else []) + [qc]
                        for kc in kcs:
                            for head in range(2):
                                v.wait_ge(sEX.h, sEX0 + n + 1)
                                mask = msame_s if kc == qc else mprev_s
                                vMS.inc(v.tensor_tensor(PT[:, n % 4, :], PT[:, n % 4, :],
                                                        mask[:], ALU.mult))
                                n += 1
                        outb = [pb[6], pb[7]] if qc % 2 == 0 else [pb[0], pb[1]]
                        v.wait_ge(tPV.h, pv_targets[n - 1])
                        for head in range(2):
                            v.reciprocal(rcol[:, head:head + 1], outb[head][:, 64:65])
                        for head in range(2):
                            vNM.inc(v.tensor_scalar(anat[:, qc, 80 * head:80 * head + 64],
                                                    outb[head][:, 0:64],
                                                    rcol[:, head:head + 1], None, ALU.mult))
                vNM_j = vNM.n

                @block.sync
                def _(s, j=j, vNM_j=vNM_j):
                    s.wait_ge(vNM.h, vNM_j)
                    qL.inc(s.dma_start(out=anat_h[j].rearrange("p (t c) -> p t c", c=160),
                                       in_=anat[:]), 16)
                    dstP = anatP_h[j].rearrange("(t p) c -> p t c", p=128)
                    for q4 in range(4):
                        qL.inc(s.dma_start(out=dstP[:, 8 * q4:8 * q4 + 8, 0:64],
                                           in_=anat[:, 8 * q4:8 * q4 + 8, 0:64]), 16)
                        qL.inc(s.dma_start(out=dstP[:, 8 * q4:8 * q4 + 8, 64:128],
                                           in_=anat[:, 8 * q4:8 * q4 + 8, 80:144]), 16)
                anat_qs.append(qL.n)

                if "d_anat" in dbg:
                    @block.gpsimd
                    def _(g, j=j, vNM_j=vNM_j):
                        g.wait_ge(vNM.h, vNM_j)
                        qS.inc(g.dma_start(
                            out=dbg["d_anat"][j].rearrange("p (t c) -> p t c", c=160),
                            in_=anat[:]), 16)

            if "d_xnt" in dbg:
                qS_an_last = qS.n

                @block.gpsimd
                def _(g):
                    g.wait_ge(qS.h, qS_an_last)
                    qS.inc(g.dma_start(out=dbg["d_xnt"], in_=xnT[:]), 16)
                qS_dbg = qS.n

                @block.vector
                def _(v):
                    v.wait_ge(qS.h, qS_dbg)

        if upto < 2:
            @block.gpsimd
            def _(g):
                qS.wait(g)
                qL.wait(g)
            return nc
        ctxA = ExitStack()
        A_hT = ctxA.enter_context(nc.sbuf_tensor("A_hT", [128, NP, S], BF16))

        @block.sync
        def _(s):
            s.wait_ge(qL.h, anat_qs[-1])
            for j in range(NP):
                qL.inc(s.dma_start(out=A_hT[:, j, :], in_=anatP_h[j],
                                   transpose=True), 16)
        qL_aht = qL.n
        qC_aht = qC.n
        if "d_aht" in dbg:
            @block.gpsimd
            def _(g):
                g.wait_ge(qL.h, qL_aht)
                qS.inc(g.dma_start(out=dbg["d_aht"], in_=A_hT[:]), 16)

        ctx2 = ExitStack()
        with ctx2:
            s2 = lambda name, shape, dt=BF16: ctx2.enter_context(
                nc.sbuf_tensor(name, list(shape), dt))
            abuf = s2("abuf", (128, 2, 32, 160))
            kaT = s2("kaT", (128, S))
            qaT = s2("qaT", (128, 4, 512))
            wqa_b = s2("wqa_b", (128, 2, 8, 128))
            wka_b = s2("wka_b", (128, 2, 8, 128))
            outT = s2("outT", (128, NP, 2048))
            PT2 = s2("PT2", (128, 4, 512))
            bc_sb = s2("bc_sb", (128, 512), F32)
            rrow = s2("rrow", (1, 1024), F32)
            lof = s2("lof", (1, 1024), F32)
            hirow = s2("hirow", (1, 1024))
            lorow = s2("lorow", (1, 1024))

            wqa_src = d["wqaT"].rearrange("(c p) f -> p c f", p=128)
            wka_src = d["wkaT"].rearrange("(c p) f -> p c f", p=128)

            pv_after_pair = []
            kaqa_end = []

            for j in range(NP):
                jj = j % 2
                qG0 = qG.n

                @block.gpsimd
                def _(g, j=j, jj=jj):
                    if j >= 2:
                        g.wait_ge(tPS.h, kaqa_end[j - 2])
                    else:
                        g.wait_ge(qL.h, anat_qs[-1])
                    qG.inc(g.dma_start(out=wqa_b[:, jj, :, :],
                                       in_=wqa_src[:, :, 128 * j:128 * j + 128]), 16)
                    qG.inc(g.dma_start(out=wka_b[:, jj, :, :],
                                       in_=wka_src[:, :, 128 * j:128 * j + 128]), 16)

                @block.sync
                def _(s, j=j, jj=jj, qs=anat_qs[j]):
                    s.wait_ge(qL.h, qs)
                    if j >= 2:
                        s.wait_ge(tPV.h, pv_after_pair[j - 2])
                    qL.inc(s.dma_start(out=abuf[:, jj, :, :],
                                       in_=anat_h[j].rearrange("p (t c) -> p t c", c=160)), 16)
                qL_a = qL.n

                sEV0, sEX0, vMS0, vNM0, vHL0, tPV0p = (sEV.n, sEX.n, vMS.n,
                                                       vNM.n, vHL.n, tPV.n)
                rec = {}

                def emit_tensor(t, pr, j=j, jj=jj, qG0=qG0, qL_a=qL_a, sEV0=sEV0,
                                sEX0=sEX0, vMS0=vMS0, vNM0=vNM0, vHL0=vHL0):
                    R = rec.setdefault(pr, dict(ka=[], qa=[], sc=[], pv=[], bc=[]))
                    stripes = STRIPES[pr]
                    t.wait_ge(qG.h, qG0 + 32)
                    t.wait_ge(qL.h, qL_a)
                    if j == 0:
                        t.wait_ge(qL.h, qL_aht)
                        t.wait_ge(qC.h, qC_aht)
                    t.wait_ge(sEV.h, sEV0)
                    t.wait_ge(sEX.h, sEX0)
                    t.wait_ge(vNM.h, vNM0)
                    for blk in range(8):
                        if blk >= 2:
                            t.wait_ge(sEV.h, sEV0 + blk - 1)
                        ps = pb[6 + blk % 2]
                        for dc in range(8):
                            t.matmul(ps[:, 0:512], wka_b[:, jj, dc, :],
                                     A_hT[:, dc, 512 * blk:512 * blk + 512],
                                     start=(dc == 0), stop=(dc == 7))
                        tPS.inc(t.drain())
                        R["ka"].append(tPS.n)
                    for sl in range(4):
                        g = stripes[sl]
                        t.wait_ge(sEV.h, sEV0 + 8 + (sl - 1 if sl >= 2 else 0))
                        ps = pb[6 + sl % 2]
                        for dc in range(8):
                            t.matmul(ps[:, 0:512], wqa_b[:, jj, dc, :],
                                     A_hT[:, dc, 512 * g:512 * g + 512],
                                     start=(dc == 0), stop=(dc == 7))
                        tPS.inc(t.drain())
                        R["qa"].append(tPS.n)
                    n = 0
                    nd = 0
                    for sl in range(4):
                        g = stripes[sl]
                        nch = 4 * g + 4
                        t.wait_ge(vNM.h, vNM0 + 2 * sl)
                        for c in range(nch):
                            if n >= 2:
                                t.wait_ge(sEX.h, sEX0 + 2 * (n - 2) + 2)
                            for head in range(2):
                                bank = pb[(n % 2) * 2 + head]
                                t.matmul(bank[:, 0:512],
                                         kaT[64 * head:64 * head + 64, 128 * c:128 * c + 128],
                                         qaT[64 * head:64 * head + 64, sl, :],
                                         start=True, stop=True,
                                         tile_position=(64 * head, 0))
                            tSC.inc(t.drain())
                            R["sc"].append(tSC.n)
                            R["sc"].append(tSC.n)
                            is_diag = c >= 4 * g
                            for head in range(2):
                                if is_diag:
                                    nd += 1
                                    t.wait_ge(vMS.h, vMS0 + nd)
                                else:
                                    t.wait_ge(sEX.h, sEX0 + 2 * n + head + 1)
                            for head in range(2):
                                t.matmul(pb[4 + head][0:65, :],
                                         abuf[:, jj, c, 80 * head:80 * head + 65],
                                         PT2[:, (n % 2) * 2 + head, :],
                                         start=(c == 0), stop=(c == nch - 1))
                            tPV.inc(t.drain())
                            R["pv"].append(tPV.n)
                            R["pv"].append(tPV.n)
                            n += 1
                        for head in range(2):
                            t.wait_ge(vHL.h, vHL0 + 2 * sl + head + 1)
                            if sl >= 1:
                                t.wait_ge(sEV.h, sEV0 + 12 + 2 * sl)
                            ps = pb[6 + head]
                            t.matmul(ps[0:64, 0:512], o64[:], hirow[:, 512 * head:512 * head + 512],
                                     start=True, stop=False)
                            t.matmul(ps[0:64, 0:512], o64[:], lorow[:, 512 * head:512 * head + 512],
                                     start=False, stop=True)
                            tBC.inc(t.drain())
                            R["bc"].append(tBC.n)

                def emit_scalar(sc, pr, j=j, tPV0p=tPV0p):
                    R = rec[pr]
                    stripes = STRIPES[pr]
                    for blk in range(8):
                        sc.wait_ge(tPS.h, R["ka"][blk])
                        ps = pb[6 + blk % 2]
                        sEV.inc(sc.activation(kaT[:, 512 * blk:512 * blk + 512],
                                              ps[:, 0:512], AF.Identity,
                                              bias=bka_sb[:, j:j + 1]))
                    for sl in range(4):
                        sc.wait_ge(tPS.h, R["qa"][sl])
                        ps = pb[6 + sl % 2]
                        sEV.inc(sc.activation(qaT[:, sl, :], ps[:, 0:512],
                                              AF.Identity, bias=bqa_sb[:, j:j + 1]))
                    n = 0
                    for sl in range(4):
                        g = stripes[sl]
                        nch = 4 * g + 4
                        for c in range(nch):
                            for head in range(2):
                                sc.wait_ge(tSC.h, R["sc"][2 * n + head])
                                sc.wait_ge(tPV.h, tPV0p if n < 2
                                           else R["pv"][2 * (n - 2) + head])
                                bank = pb[(n % 2) * 2 + head]
                                sEX.inc(sc.activation(PT2[:, (n % 2) * 2 + head, :],
                                                      bank[:, 0:512], AF.Exp, scale=SCALE))
                            n += 1
                        for head in range(2):
                            sc.wait_ge(tBC.h, R["bc"][2 * sl + head])
                            ps = pb[6 + head]
                            sEV.inc(sc.copy(bc_sb[64 * head:64 * head + 64, :],
                                            ps[0:64, 0:512]))

                def emit_vector(v, pr, j=j, sEX0=sEX0, sEV0=sEV0):
                    R = rec[pr]
                    stripes = STRIPES[pr]
                    n = 0
                    for sl in range(4):
                        g = stripes[sl]
                        nch = 4 * g + 4
                        for c in range(nch):
                            if c >= 4 * g:
                                cr = c - 4 * g
                                for head in range(2):
                                    v.wait_ge(sEX.h, sEX0 + 2 * n + head + 1)
                                    vMS.inc(v.tensor_tensor(
                                        PT2[:, (n % 2) * 2 + head, :],
                                        PT2[:, (n % 2) * 2 + head, :],
                                        diag_s[:, cr, :], ALU.mult))
                            n += 1
                        v.wait_ge(tPV.h, R["pv"][2 * n - 1])
                        for head in range(2):
                            v.reciprocal(rrow[:, 512 * head:512 * head + 512], pb[4 + head][64:65, :])
                        for head in range(2):
                            v.tensor_copy(hirow[:, 512 * head:512 * head + 512], rrow[:, 512 * head:512 * head + 512])
                        for head in range(2):
                            v.tensor_tensor(lof[:, 512 * head:512 * head + 512], rrow[:, 512 * head:512 * head + 512],
                                            hirow[:, 512 * head:512 * head + 512], ALU.subtract)
                        for head in range(2):
                            vHL.inc(v.tensor_copy(lorow[:, 512 * head:512 * head + 512],
                                                  lof[:, 512 * head:512 * head + 512]))
                        for head in range(2):
                            v.wait_ge(sEV.h, sEV0 + 12 + 2 * sl + 2)
                            vNM.inc(v.tensor_tensor(
                                outT[64 * head:64 * head + 64, j, 512 * sl:512 * sl + 512],
                                pb[4 + head][0:64, :],
                                bc_sb[64 * head:64 * head + 64, :], ALU.mult))

                def run_branched(decorator, fn):
                    def body(eng):
                        before = {id(c): c.n for c in CTRS}
                        with eng.If_eq(par_reg[eng.engine], 0):
                            fn(eng, 0)
                        after0 = {id(c): c.n for c in CTRS}
                        for c in CTRS:
                            c.n = before[id(c)]
                        with eng.If_eq(par_reg[eng.engine], 1):
                            fn(eng, 1)
                        for c in CTRS:
                            assert c.n == after0[id(c)], "branch sem count mismatch"
                    decorator(body)

                run_branched(block.tensor, emit_tensor)
                run_branched(block.scalar, emit_scalar)
                run_branched(block.vector, emit_vector)
                pv_after_pair.append(tPV.n)
                kaqa_end.append(tPS.n)

            vNM_all = vNM.n

            @block.gpsimd
            def _(g):
                g.wait_ge(vNM.h, vNM_all)
                qS.inc(g.dma_start(out=outT_h, in_=outT[:]), 16)
                if "d_outT" in dbg:
                    qS.inc(g.dma_start(out=dbg["d_outT"], in_=outT[:]), 16)
                if "d_kat" in dbg:
                    qS.inc(g.dma_start(out=dbg["d_kat"], in_=kaT[:]), 16)
                if "d_qat" in dbg:
                    qS.inc(g.dma_start(out=dbg["d_qat"],
                                       in_=qaT[:].rearrange("p a b -> p (a b)")), 16)
            qS_outT = qS.n

            @block.vector
            def _(v):
                v.wait_ge(qS.h, qS_outT)

        ctxA.close()

        if upto < 3:
            @block.gpsimd
            def _(g):
                qS.wait(g)
                qL.wait(g)
            return nc
        ctx3 = ExitStack()
        with ctx3:
            R3_BD, R3_SB2, R3_RB2, R3_RTB = 0, 1024, 2048, 2048 + NE * 1024
            R3L = 2048 + NE * 1024 + NE
            rows3_sb = ctx3.enter_context(nc.sbuf_tensor("rows3_sb", [1, R3L], BF16))

            @block.gpsimd
            def _(g):
                qG.inc(g.dma_start(out=rows3_sb[:, R3_BD:R3_BD + 1024], in_=d["bd_row"]), 16)
                qG.inc(g.dma_start(out=rows3_sb[:, R3_SB2:R3_SB2 + 1024], in_=d["sb2_row"]), 16)
                for e in range(NE):
                    qG.inc(g.dma_start(out=rows3_sb[:, R3_RB2 + 1024 * e:R3_RB2 + 1024 * e + 1024],
                                       in_=d["rb2"][e:e + 1, :]), 16)
                qG.inc(g.dma_start(out=rows3_sb[:, R3_RTB:R3_RTB + NE], in_=d["router_b_row"]), 16)
            qG_rows3 = qG.n

            y_acc = ctx3.enter_context(nc.sbuf_tensor("y_acc", [128, 8, DM], F32))
            xnT2 = ctx3.enter_context(nc.sbuf_tensor("xnT2", [128, 8, 1024], BF16))
            xm_b = ctx3.enter_context(nc.sbuf_tensor("xm_b", [128, 2, DM], F32))
            rprob = ctx3.enter_context(nc.sbuf_tensor("rprob", [128, 8, NE], F32))
            rden = ctx3.enter_context(nc.sbuf_tensor("rden", [128, 8], F32))
            rwork = ctx3.enter_context(nc.sbuf_tensor("rwork", [128, 8, NE], F32))
            rm1 = ctx3.enter_context(nc.sbuf_tensor("rm1", [128, 8], F32))
            rm2 = ctx3.enter_context(nc.sbuf_tensor("rm2", [128, 8], F32))
            xm_src = d["x_mine"].rearrange("(t p) m -> p t m", p=128)
            out_dst = out_h.rearrange("(t p) m -> p t m", p=128)
            xn2_dst = xn2_h.rearrange("(t p) m -> p t m", p=128)

            for half in range(2):
                qS_prev_out = qS.n
                with ExitStack() as ctxWd:
                    outTh = ctxWd.enter_context(nc.sbuf_tensor(f"outTh{half}", [128, 8, 1024], BF16))
                    wdT_b = ctxWd.enter_context(nc.sbuf_tensor(f"wdT_b{half}", [128, 8, 1024], BF16))
                    qL0, qG0, vEV0 = qL.n, qG.n, vEV.n

                    @block.sync
                    def _(s, half=half, vEV0=vEV0):
                        s.wait_ge(qS.h, qS_outT)
                        vEV.wait(s)
                        sEV.wait(s)
                        qL.inc(s.dma_start(out=outTh[:],
                                           in_=outT_h[:, :, 1024 * half:1024 * half + 1024]), 16)
                        for tt in range(8):
                            if tt >= 2:
                                s.wait_ge(vEV.h, vEV0 + 2 * (tt - 2) + 2)
                            qL.inc(s.dma_start(out=xm_b[:, tt % 2, :],
                                               in_=xm_src[:, 8 * half + tt, :]), 16)

                    @block.gpsimd
                    def _(g):
                        qG.inc(g.dma_start(out=wdT_b[:],
                                           in_=d["wdT"].rearrange("(c p) f -> p c f", p=128)), 16)

                    wd_targets = []

                    @block.tensor
                    def _(t, qL0=qL0, qG0=qG0, vEV0=vEV0):
                        t.wait_ge(qL.h, qL0 + 16)
                        t.wait_ge(qG.h, qG0 + 16)
                        t.wait_ge(qG.h, qG_rows3)
                        sEX.wait(t)
                        vNM.wait(t)
                        for gi in range(16):
                            tt, dh = gi // 2, gi % 2
                            if gi >= 2:
                                t.wait_ge(vEV.h, vEV0 + gi - 1)
                            ps = pb[gi % 2]
                            for fc in range(8):
                                t.matmul(ps[:, 0:512], outTh[:, fc, 128 * tt:128 * tt + 128],
                                         wdT_b[:, fc, 512 * dh:512 * dh + 512],
                                         start=(fc == 0), stop=False)
                            t.matmul(ps[:, 0:512], o128[:],
                                     rows3_sb[:, R3_BD + 512 * dh:R3_BD + 512 * dh + 512],
                                     start=False, stop=True)
                            tPS.inc(t.drain())
                            wd_targets.append(tPS.n)

                    @block.vector
                    def _(v, qL0=qL0, qS_prev_out=qS_prev_out):
                        v.wait_ge(qS.h, qS_prev_out)
                        for gi in range(16):
                            tt, dh = gi // 2, gi % 2
                            v.wait_ge(tPS.h, wd_targets[gi])
                            v.wait_ge(qL.h, qL0 + 16 * (tt + 2))
                            ps = pb[gi % 2]
                            vEV.inc(v.tensor_tensor(y_acc[:, tt, 512 * dh:512 * dh + 512],
                                                    ps[:, 0:512],
                                                    xm_b[:, tt % 2, 512 * dh:512 * dh + 512],
                                                    ALU.add))

                with ExitStack() as ctxL2:
                    c2 = lambda name, shape, dt: ctxL2.enter_context(
                        nc.sbuf_tensor(name + str(half), list(shape), dt))
                    xnb2 = c2("xnb2", (128, 4, DM), BF16)
                    trash2 = c2("trash2", (128, 4, DM), BF16)
                    mcol2 = c2("mcol2", (128, 4), F32)
                    msqc2 = c2("msqc2", (128, 4), F32)
                    ssqc2 = c2("ssqc2", (128, 4), F32)
                    varc2 = c2("varc2", (128, 4), F32)
                    sqvc2 = c2("sqvc2", (128, 4), F32)
                    rstdc2 = c2("rstdc2", (128, 4), F32)
                    vEV_wd = vEV.n
                    qS_l2 = qL.n

                    for gidx in range(2):
                        tiles = [4 * gidx + i for i in range(4)]
                        sLN0, vMS0 = sLN.n, vMS.n

                        @block.scalar
                        def _(sc, tiles=tiles, vEV_wd=vEV_wd):
                            sc.wait_ge(vEV.h, vEV_wd)
                            for i, t in enumerate(tiles):
                                sLN.inc(sc.activation(trash2[:, i, :], y_acc[:, t, :], AF.Square,
                                                      accum_out=ssqc2[:, i:i + 1]))

                        @block.vector
                        def _(v, tiles=tiles, sLN0=sLN0, vEV_wd=vEV_wd):
                            v.wait_ge(vEV.h, vEV_wd)
                            for i, t in enumerate(tiles):
                                v.reduce_sum(mcol2[:, i:i + 1], y_acc[:, t, :], axis=AX.X)
                            for i in range(4):
                                v.tensor_scalar_mul(mcol2[:, i:i + 1], mcol2[:, i:i + 1], 1.0 / DM)
                            for i in range(4):
                                v.tensor_tensor(msqc2[:, i:i + 1], mcol2[:, i:i + 1],
                                                mcol2[:, i:i + 1], ALU.mult)
                            v.wait_ge(sLN.h, sLN0 + 4)
                            for i in range(4):
                                v.scalar_tensor_tensor(varc2[:, i:i + 1], ssqc2[:, i:i + 1],
                                                       1.0 / DM, msqc2[:, i:i + 1],
                                                       ALU.mult, ALU.subtract)
                            for i in range(4):
                                vMS.inc(v.tensor_scalar_add(varc2[:, i:i + 1],
                                                            varc2[:, i:i + 1], 1e-5))

                        @block.scalar
                        def _(sc, vMS0=vMS0):
                            sc.wait_ge(vMS.h, vMS0 + 4)
                            for i in range(4):
                                sLN.inc(sc.activation(sqvc2[:, i:i + 1], varc2[:, i:i + 1],
                                                      AF.Sqrt))

                        @block.vector
                        def _(v, tiles=tiles, sLN0=sLN0, gidx=gidx, qS_l2=qS_l2):
                            v.wait_ge(sLN.h, sLN0 + 8)
                            if gidx == 1:
                                v.wait_ge(qL.h, qS_l2 + 16 * 4)
                            for i in range(4):
                                v.reciprocal(rstdc2[:, i:i + 1], sqvc2[:, i:i + 1])
                            for i, t in enumerate(tiles):
                                vEV.inc(v.tensor_scalar(xnb2[:, t % 4, :], y_acc[:, t, :],
                                                        mcol2[:, i:i + 1], rstdc2[:, i:i + 1],
                                                        ALU.subtract, ALU.mult))

                        vEVg = vEV.n

                        @block.sync
                        def _(s, tiles=tiles, vEVg=vEVg, half=half):
                            for i, t in enumerate(tiles):
                                s.wait_ge(vEV.h, vEVg - 3 + i)
                                qL.inc(s.dma_start(out=xn2_dst[:, 8 * half + t, :],
                                                   in_=xnb2[:, t % 4, :]), 16)

                    qS_x2 = qL.n

                    @block.sync
                    def _(s, half=half):
                        s.wait_ge(qL.h, qS_x2)
                        for f in range(8):
                            qL.inc(s.dma_start(out=xnT2[:, f, :],
                                               in_=xn2_h[1024 * half:1024 * half + 1024,
                                                         128 * f:128 * f + 128],
                                               transpose=True), 16)
                qL_x2t = qL.n

                rt_targets = []
                sEX0r = sEX.n

                @block.tensor
                def _(t):
                    t.wait_ge(qL.h, qL_x2t)
                    t.wait_ge(qG.h, qG_P0)
                    for tt in range(8):
                        if tt >= 2:
                            t.wait_ge(sEX.h, sEX0r + tt - 1)
                        ps = pb[2 + tt % 2]
                        for dc in range(8):
                            t.matmul(ps[:, 0:NE], xnT2[:, dc, 128 * tt:128 * tt + 128],
                                     rwr_sb[:, dc, :], start=(dc == 0), stop=False)
                        t.matmul(ps[:, 0:NE], o128[:], rows3_sb[:, R3_RTB:R3_RTB + NE],
                                 start=False, stop=True)
                        tPS.inc(t.drain())
                        rt_targets.append(tPS.n)

                @block.scalar
                def _(sc):
                    for tt in range(8):
                        sc.wait_ge(tPS.h, rt_targets[tt])
                        ps = pb[2 + tt % 2]
                        sEX.inc(sc.activation(rprob[:, tt, :], ps[:, 0:NE], AF.Exp,
                                              accum_out=rden[:, tt:tt + 1]))
                sEX_rt = sEX.n

                @block.vector
                def _(v):
                    v.wait_ge(sEX.h, sEX_rt)
                    vIN.wait(v)
                    for tt in range(8):
                        v.reciprocal(rm1[:, tt:tt + 1], rden[:, tt:tt + 1])
                    for tt in range(8):
                        v.tensor_scalar(rprob[:, tt, :], rprob[:, tt, :],
                                        rm1[:, tt:tt + 1], None, ALU.mult)
                    for tt in range(8):
                        v.reduce_max(rm1[:, tt:tt + 1], rprob[:, tt, :], axis=AX.X)
                    for tt in range(8):
                        v.tensor_scalar(rwork[:, tt, :], rprob[:, tt, :],
                                        rm1[:, tt:tt + 1], None, ALU.is_ge)
                    for tt in range(8):
                        v.scalar_tensor_tensor(rwork[:, tt, :], rwork[:, tt, :], -1e9,
                                               rprob[:, tt, :], ALU.mult, ALU.add)
                    for tt in range(8):
                        v.reduce_max(rm2[:, tt:tt + 1], rwork[:, tt, :], axis=AX.X)
                    for tt in range(8):
                        v.tensor_scalar(rwork[:, tt, :], rprob[:, tt, :],
                                        rm2[:, tt:tt + 1], None, ALU.is_ge)
                    for tt in range(8):
                        vIN.inc(v.tensor_tensor(gates_sb[:, tt, 0:NE], rprob[:, tt, :],
                                                rwork[:, tt, :], ALU.mult))
                vIN_gt = vIN.n

                if "d_gates" in dbg and half == 0:
                    @block.gpsimd
                    def _(g, vIN_gt=vIN_gt):
                        g.wait_ge(vIN.h, vIN_gt)
                        qS.inc(g.dma_start(out=dbg["d_gates"], in_=gates_sb[:]), 16)

                tPS_pre_ex = wd_targets[-1]
                vEV_pre_ex = vEV.n
                qL_pre_ex = qL.n
                with ExitStack() as ctxEx:
                    w1b = ctxEx.enter_context(nc.sbuf_tensor(f"w1b{half}", [128, 2, 8, 1024], BF16))
                    w2b = ctxEx.enter_context(nc.sbuf_tensor(f"w2b{half}", [128, 2, 8, 1024], BF16))
                    hT = ctxEx.enter_context(nc.sbuf_tensor(f"hT{half}", [128, 8, 1024], BF16))

                    mm2_after_step = []
                    for step in range(18):
                        e, hh = step // 2, step % 2
                        ss = step % 2

                        @block.gpsimd
                        def _(g, e=e, hh=hh, ss=ss, step=step):
                            if step >= 2:
                                g.wait_ge(tPS.h, mm2_after_step[step - 2])
                            else:
                                g.wait_ge(tPS.h, tPS_pre_ex)
                                g.wait_ge(vEV.h, vEV_pre_ex)
                                g.wait_ge(qL.h, qL_pre_ex)
                            if e < NE:
                                w1s = d["rw1"][e].rearrange("(c p) f -> p c f", p=128)[:, :, 1024 * hh:1024 * hh + 1024]
                                w2s = d["rw2"][e][1024 * hh:1024 * hh + 1024].rearrange("(c p) f -> p c f", p=128)
                            else:
                                w1s = d["sw1"].rearrange("(c p) f -> p c f", p=128)[:, :, 1024 * hh:1024 * hh + 1024]
                                w2s = d["sw2"][1024 * hh:1024 * hh + 1024].rearrange("(c p) f -> p c f", p=128)
                            qG.inc(g.dma_start(out=w1b[:, ss, :, :], in_=w1s), 16)
                            qG.inc(g.dma_start(out=w2b[:, ss, :, :], in_=w2s), 16)

                        gl_targets = []
                        m2_targets = []
                        sEV0e, vEV0e = sEV.n, vEV.n
                        qG1 = qG.n

                        @block.tensor
                        def _(t, e=e, hh=hh, ss=ss, qG1=qG1, sEV0e=sEV0e, vEV0e=vEV0e):
                            t.wait_ge(qG.h, qG1)
                            sEX.wait(t)
                            for gi in range(16):
                                ht, b2 = gi // 2, gi % 2
                                t.wait_ge(sEV.h, sEV0e + (gi - 1 if gi >= 2 else 0))
                                ps = pb[gi % 2]
                                for dc in range(8):
                                    t.matmul(ps[:, 0:512],
                                             w1b[:, ss, dc, 128 * ht:128 * ht + 128],
                                             xnT2[:, dc, 512 * b2:512 * b2 + 512],
                                             start=(dc == 0), stop=(dc == 7))
                                tPS.inc(t.drain())
                                gl_targets.append(tPS.n)
                            t.wait_ge(sEV.h, sEV0e + 16)
                            for gi in range(16):
                                tt, dh = gi // 2, gi % 2
                                t.wait_ge(vEV.h, vEV0e + (gi - 1 if gi >= 2 else 0))
                                ps = pb[2 + gi % 2]
                                for hc in range(8):
                                    t.matmul(ps[:, 0:512], hT[:, hc, 128 * tt:128 * tt + 128],
                                             w2b[:, ss, hc, 512 * dh:512 * dh + 512],
                                             start=(hc == 0), stop=(hh == 0 and hc == 7))
                                if hh == 1:
                                    if e < NE:
                                        brow = rows3_sb[:, R3_RB2 + 1024 * e + 512 * dh:
                                                        R3_RB2 + 1024 * e + 512 * dh + 512]
                                    else:
                                        brow = rows3_sb[:, R3_SB2 + 512 * dh:R3_SB2 + 512 * dh + 512]
                                    t.matmul(ps[:, 0:512], o128[:], brow,
                                             start=False, stop=True)
                                tPS.inc(t.drain())
                                m2_targets.append(tPS.n)
                        mm2_after_step.append(tPS.n)

                        @block.scalar
                        def _(sc, e=e, hh=hh):
                            for gi in range(16):
                                ht, b2 = gi // 2, gi % 2
                                sc.wait_ge(tPS.h, gl_targets[gi])
                                ps = pb[gi % 2]
                                if e < NE:
                                    bias = rb1_sb[:, e, 8 * hh + ht:8 * hh + ht + 1]
                                else:
                                    bias = sb1_sb[:, 8 * hh + ht:8 * hh + ht + 1]
                                sEV.inc(sc.activation(hT[:, ht, 512 * b2:512 * b2 + 512],
                                                      ps[:, 0:512], AF.Gelu, bias=bias))

                        @block.vector
                        def _(v, e=e, vIN_gt=vIN_gt):
                            v.wait_ge(vIN.h, vIN_gt)
                            gcol = min(e, 8)
                            for gi in range(16):
                                tt, dh = gi // 2, gi % 2
                                v.wait_ge(tPS.h, m2_targets[gi])
                                ps = pb[2 + gi % 2]
                                vEV.inc(v.scalar_tensor_tensor(
                                    y_acc[:, tt, 512 * dh:512 * dh + 512], ps[:, 0:512],
                                    gates_sb[:, tt, gcol:gcol + 1],
                                    y_acc[:, tt, 512 * dh:512 * dh + 512],
                                    ALU.mult, ALU.add))

                vEV_fin = vEV.n

                @block.gpsimd
                def _(g, half=half, vEV_fin=vEV_fin):
                    g.wait_ge(vEV.h, vEV_fin)
                    qS.inc(g.dma_start(out=out_dst[:, 8 * half:8 * half + 8, :],
                                       in_=y_acc[:]), 16)
                    if "d_yacc" in dbg and half == 0:
                        qS.inc(g.dma_start(out=dbg["d_yacc"], in_=y_acc[:]), 16)

        @block.gpsimd
        def _(g):
            qS.wait(g)
            qL.wait(g)

    return nc


def _in_maps(inputs):
    x = np.ascontiguousarray(inputs["L0_attn_hidden_states"], dtype=np.float32)
    w = {k: np.ascontiguousarray(v, dtype=np.float32) for k, v in inputs.items()}
    shared = {
        "wqkvT": np.ascontiguousarray(w["L0_attn_Wqkv"].T),
        "bqkv_pc": np.ascontiguousarray(w["L0_attn_bqkv"].reshape(24, 128).T),
        "bqkv_row": w["L0_attn_bqkv"].reshape(1, -1),
        "wqaT": np.ascontiguousarray(w["L0_attn_Wqa"].T),
        "bqa_pc": np.ascontiguousarray(w["L0_attn_bqa"].reshape(8, 128).T),
        "wkaT": np.ascontiguousarray(w["L0_attn_Wka"].T),
        "bka_pc": np.ascontiguousarray(w["L0_attn_bka"].reshape(8, 128).T),
        "wdT": np.ascontiguousarray(w["L0_attn_Wd"].T),
        "bd_row": w["L0_attn_bd"].reshape(1, -1),
        "router_w_pc": np.ascontiguousarray(w["L0_ffn_router_w"].reshape(8, 128, NE).transpose(1, 0, 2)),
        "router_b_row": w["L0_ffn_router_b"].reshape(1, -1),
        "sw1": w["L0_ffn_sw1"],
        "sb1_pc": np.ascontiguousarray(w["L0_ffn_sb1"].reshape(16, 128).T),
        "sw2": w["L0_ffn_sw2"],
        "sb2_row": w["L0_ffn_sb2"].reshape(1, -1),
        "rw1": w["L0_ffn_rw1"],
        "rb1_pc": np.ascontiguousarray(w["L0_ffn_rb1"].reshape(NE, 16, 128).transpose(2, 0, 1)),
        "rw2": w["L0_ffn_rw2"],
        "rb2": w["L0_ffn_rb2"],
    }
    maps = []
    for c in range(8):
        b, p = c // 2, c % 2
        stripes = STRIPES[p]
        xm = np.ascontiguousarray(x[b].reshape(8, 512, DM)[stripes].reshape(2048, DM))
        m = dict(shared)
        m["x_b"] = x[b]
        m["x_mine"] = xm
        maps.append(m)
    return maps


def kernel(**inputs):
    nc = build_program(debug=_DEBUG)
    maps = _in_maps(inputs)
    res = run_bass_kernel_spmd(nc, maps, core_ids=list(range(8)))
    y = np.zeros((4, S, DM), dtype=np.float32)
    for c in range(8):
        b, p = c // 2, c % 2
        shard = np.asarray(res.results[c]["out"]).reshape(4, 512, DM)
        for i, g in enumerate(STRIPES[p]):
            y[b, 512 * g:512 * g + 512] = shard[i]
    return y
